# version 05
# baseline (speedup 1.0000x reference)
import math
import numpy as np
from contextlib import ExitStack

import concourse.bass as bass
import concourse.mybir as mybir
from concourse.bass_utils import run_bass_kernel_spmd

F32 = mybir.dt.float32
BF16 = mybir.dt.bfloat16
I32 = mybir.dt.int32
AF = mybir.ActivationFunctionType
ALU = mybir.AluOpType
AX = mybir.AxisListType

NCORE = 8
B_, N_, D_ = 4, 8192, 1024
H_, DH_, M_ = 16, 64, 256
E_, F_ = 8, 4096
DEBUG = 0
TRUNC = 0

LN_EPS = 1e-5
EPS_PERF = 1e-4
BIGOOB = 1.0e6

_CACHE = {}


def build(debug=0, N=None, trunc=0):
    N = N_ if N is None else N
    D, H, DH, M, E, F = D_, H_, DH_, M_, E_, F_
    T = B_ * N
    TC = T // NCORE
    CAP = T // E
    NT = TC // 128
    NTB = TC // 512
    DC = D // 128
    NP = H // 2
    NCH = T // 128
    TBLF = T // 128
    NTT = TBLF // 128
    NG = CAP // 256
    FC = F // 128
    NMC = M // 128
    DN = float(DH) ** -0.25
    RATIO = float(M) ** -0.5
    LNR = math.log(RATIO)
    assert NTT >= 1

    nc = bass.Bass("TRN2", target_bir_lowering=False, debug=False)

    x_d = nc.dram_tensor("x", [TC, D], F32, kind="ExternalInput").ap()
    wq_d = nc.dram_tensor("wq", [D, D], F32, kind="ExternalInput").ap()
    wk_d = nc.dram_tensor("wk", [D, D], F32, kind="ExternalInput").ap()
    wv_d = nc.dram_tensor("wv", [D, D], F32, kind="ExternalInput").ap()
    wo_d = nc.dram_tensor("wo", [D, D], F32, kind="ExternalInput").ap()
    bo_d = nc.dram_tensor("bo", [1, D], F32, kind="ExternalInput").ap()
    p_d = nc.dram_tensor("pmat", [M, DH], F32, kind="ExternalInput").ap()
    wr_d = nc.dram_tensor("wr", [D, E], F32, kind="ExternalInput").ap()
    w1_d = nc.dram_tensor("w1", [D, F], F32, kind="ExternalInput").ap()
    w2_d = nc.dram_tensor("w2", [F, D], F32, kind="ExternalInput").ap()
    cid_d = nc.dram_tensor("cid", [1, 1], F32, kind="ExternalInput").ap()
    y_d = nc.dram_tensor("y", [TC, D], F32, kind="ExternalOutput").ap()

    xn_dram = nc.dram_tensor("xn_dram", [TC, D], BF16, kind="Internal").ap()
    xnall = nc.dram_tensor("xnall", [T, D], BF16, kind="Internal").ap()
    msgA_in = nc.dram_tensor("msgA_in", [DH + 1, H * M + H], F32, kind="Internal").ap()
    msgA_out = nc.dram_tensor("msgA_out", [DH + 1, H * M + H], F32, kind="Internal").ap()
    MBW = 16 + TC + TC
    msgB_in = nc.dram_tensor("msgB_in", [1, MBW], F32, kind="Internal").ap()
    msgB_out = nc.dram_tensor("msgB_out", [NCORE, MBW], F32, kind="Internal").ap()
    ye_dram = nc.dram_tensor("ye_dram", [CAP, D], BF16, kind="Internal").ap()
    yeall = nc.dram_tensor("yeall", [T, D], BF16, kind="Internal").ap()
    y1_dram = nc.dram_tensor("y1_dram", [TC, D], F32, kind="Internal").ap()

    dbg = {}
    if debug >= 1:
        dbg["eidx"] = nc.dram_tensor("dbg_eidx", [128, NT], F32, kind="ExternalOutput").ap()
        dbg["ptok"] = nc.dram_tensor("dbg_ptok", [128, NT], F32, kind="ExternalOutput").ap()
        dbg["xn"] = nc.dram_tensor("dbg_xn", [TC, D], F32, kind="ExternalOutput").ap()
    if debug >= 2:
        dbg["ctx"] = nc.dram_tensor("dbg_ctx", [DH + 1, H * M + H], F32, kind="ExternalOutput").ap()
        dbg["mx"] = nc.dram_tensor("dbg_mx", [1, 1], F32, kind="ExternalOutput").ap()
    if debug >= 3:
        dbg["y1"] = nc.dram_tensor("dbg_y1", [TC, D], F32, kind="ExternalOutput").ap()
    if debug >= 4:
        dbg["dtab"] = nc.dram_tensor("dbg_dtab", [64, 256], F32, kind="ExternalOutput").ap()
        dbg["rankm"] = nc.dram_tensor("dbg_rankm", [128, NT], F32, kind="ExternalOutput").ap()

    stack = ExitStack()
    with stack:
        def sbuf(name, shape, dt):
            return stack.enter_context(nc.sbuf_tensor(name, shape, dt))

        class Sem:
            __slots__ = ("h", "n")

            def __init__(self, name):
                self.h = stack.enter_context(nc.semaphore(name))
                self.n = 0

            def inc(self, inst, k=1):
                inst.then_inc(self.h, k)
                self.n += k
                return self.n

            def wait(self, eng, n=None):
                eng.wait_ge(self.h, self.n if n is None else n)

        block = stack.enter_context(nc.Block())

        class EngP:
            __slots__ = ("_e", "last")

            def __init__(self, e):
                self._e = e
                self.last = None

            def __getattr__(self, name):
                f = getattr(self._e, name)
                if not callable(f) or name in ("nop", "wait_ge"):
                    return f

                def w(*a, **k):
                    r = f(*a, **k)
                    self.last = r
                    return r
                return w

        def on(engname, fn):
            getattr(block, engname)(lambda e: fn(EngP(e)))

        bank = {}
        for i in (0, 1, 2, 3, 5, 7):
            bank[i] = stack.enter_context(nc.psum_tensor(f"pb{i}", [128, 512], F32)).ap()
        bkb = {}
        for i in (4, 6):
            bkb[i] = stack.enter_context(nc.psum_tensor(f"pb{i}", [128, 1024], BF16)).ap()
        bkf = {i: bkb[i].bitcast(F32) for i in (4, 6)}

        sBKP = [Sem(f"sBKP{i}") for i in range(8)]
        sBKC = [Sem(f"sBKC{i}") for i in range(8)]

        def bfree(eng, i):
            eng.wait_ge(sBKC[i].h, sBKC[i].n)

        sTE = Sem("sTE")
        sVE = Sem("sVE")
        sAE = Sem("sAE")
        sGE = Sem("sGE")
        sWL = Sem("sWL")
        sXN = Sem("sXN")
        sST = Sem("sST")
        sSY1 = Sem("sSY1")
        sSYE = Sem("sSYE")
        sMSG = Sem("sMSG")
        sCC1 = Sem("sCC1")
        sCC2 = Sem("sCC2")
        sCC3 = Sem("sCC3")
        sCC4 = Sem("sCC4")
        sMGL = Sem("sMGL")
        sGA = [Sem("sGA0"), Sem("sGA1")]
        sGB = [Sem("sGB0"), Sem("sGB1")]
        sGC = Sem("sGC")
        sOUT = Sem("sOUT")
        sDBG = Sem("sDBG")
        sDBG2 = Sem("sDBG2")
        sXL = [Sem("sXL0"), Sem("sXL1")]
        sWQ = [Sem("sWQ0"), Sem("sWQ1")]
        sW1 = Sem("sW1")
        sXR = [Sem("sXR0"), Sem("sXR1")]
        sY1L = [Sem("sY1L0"), Sem("sY1L1")]

        AB1 = sbuf("AB1", [128, max(DC * TC, DC * F)], BF16)
        AB2C = max(NP * TC, 32 * D, 2 * (4 * H * 65) + DC * D)
        AB2 = sbuf("AB2", [128, AB2C], BF16)
        AB3 = sbuf("AB3", [128, 16 * 1024], BF16)
        AF1 = sbuf("AF1", [128, 3 * D], F32)
        AF1b = sbuf("AF1b", [128, D], F32)
        AF1c = sbuf("AF1c", [128, D], F32)
        AFC = sbuf("AFC", [128, max(H * M, 11 * TBLF + 4 * (TC // 128) + 1024)], F32)

        xnT = AB1.ap()
        w1_sb = AB1.ap()
        attn_feat = AB2.ap()
        w2_sb = AB2.ap()[:, 0:32 * D]
        vp = [AB2.ap()[:, i * (4 * H * 65):(i + 1) * (4 * H * 65)] for i in range(2)]
        wkfull = AB2.ap()[:, 2 * (4 * H * 65):2 * (4 * H * 65) + DC * D]
        _ab2f = 2 * (4 * H * 65) + DC * D
        kblk = [AB2.ap()[:, _ab2f + i * 512: _ab2f + (i + 1) * 512] for i in range(2)]
        kp_sb = [AB2.ap()[:, _ab2f + 1024 + i * 512: _ab2f + 1024 + (i + 1) * 512] for i in range(2)]
        k2b = AB2.ap()[:, _ab2f + 2048: _ab2f + 2560]
        wv_sb = AB3.ap()[:, 0:DC * D]
        wo_sb = AB3.ap()[:, 0:NP * D]
        qblk = AB3.ap()[:, 0:TC]
        qp_sb = [AB3.ap()[:, TC + i * 512: TC + (i + 1) * 512] for i in range(2)]
        qpT = [AB3.ap()[:, TC + 1024 + i * 512: TC + 1024 + (i + 1) * 512] for i in range(2)]
        headp = AB3.ap()[:, TC + 2048: TC + 2176]
        wq_sb = [AB3.ap()[:, TC + 2304 + i * D: TC + 2304 + (i + 1) * D] for i in range(2)]
        ctxT = AB3.ap()[:, TC + 2304 + 2 * D: TC + 2304 + 2 * D + H * NMC * 65]
        colrow = AB3.ap()[0:1, TC + 2304 + 2 * D + H * NMC * 65:
                          TC + 2304 + 2 * D + H * NMC * 65 + H * 65]
        bo_row = AF1b.ap()[0:1, 0:D]
        diagp = AF1c.ap()[0:2, 0:512]
        xloT = [AB3.ap()[:, 12288 + i * D: 12288 + (i + 1) * D] for i in range(2)]
        xnbf = [AB3.ap()[:, 14336 + i * D: 14336 + (i + 1) * D] for i in range(2)]
        hbuf = AB3.ap()[:, 0:32 * 256]
        xeT = AB3.ap()[:, 8192:8192 + DC * 256]
        xe_tok = [AB3.ap()[:, 10240 + i * D: 10240 + (i + 1) * D] for i in range(2)]
        yebuf = [AB3.ap()[:, 12288 + i * D: 12288 + (i + 1) * D] for i in range(2)]
        yegb = [AB3.ap()[:, 14336 + i * D: 14336 + (i + 1) * D] for i in range(2)]
        xbuf = [AF1.ap()[:, 0:D], AF1.ap()[:, D:2 * D]]
        xnfp = AF1.ap()[:, 2 * D:3 * D]
        xrbuf = [AF1.ap()[:, 0:D], AF1.ap()[:, D:2 * D]]
        y1buf = [AF1.ap()[:, 2 * D:3 * D], AF1c.ap()[:, 0:D]]
        borep = AF1b.ap()[:, 0:D]
        ctx_acc = AFC.ap()[0:DH + 1, :]
        ctxF = ctx_acc
        rankE = AFC.ap()[:, 0:TBLF]
        hiE = AFC.ap()[:, TBLF:2 * TBLF]
        loE = AFC.ap()[:, 2 * TBLF:3 * TBLF]
        hiT = AFC.ap()[:, 3 * TBLF:4 * TBLF]
        loT = AFC.ap()[:, 4 * TBLF:5 * TBLF]
        gTT = AFC.ap()[:, 5 * TBLF:6 * TBLF]
        scrA = AFC.ap()[:, 6 * TBLF:7 * TBLF]
        scrB = AFC.ap()[:, 7 * TBLF:8 * TBLF]
        eidxall = AFC.ap()[:, 8 * TBLF:9 * TBLF]
        ptokall = AFC.ap()[:, 9 * TBLF:10 * TBLF]
        maskE = AFC.ap()[:, 10 * TBLF:11 * TBLF]
        rmine = AFC.ap()[:, 11 * TBLF:11 * TBLF + NT]
        rk_u = AFC.ap()[:, 11 * TBLF + NT:11 * TBLF + 2 * NT]
        rk_t = AFC.ap()[:, 11 * TBLF + 2 * NT:11 * TBLF + 3 * NT]
        rk_s = AFC.ap()[:, 11 * TBLF + 3 * NT:11 * TBLF + 4 * NT]
        _cf = 11 * TBLF + 4 * NT
        def _take(n):
            nonlocal _cf
            a = _cf
            _cf += n
            return a
        _a = _take(256); dtab = AFC.ap()[0:64, _a:_a + 256]
        _a = _take(64); dfull = AFC.ap()[0:64, _a:_a + 64]
        _a = _take(64); dfT = AFC.ap()[0:64, _a:_a + 64]
        _a = _take(64); gcq = AFC.ap()[0:64, _a:_a + 64]
        _a = _take(TBLF // 2); maskEb = AFC.ap()[:, _a:_a + TBLF // 2].bitcast(BF16)
        _a = _take(32); ohhi = AFC.ap()[:, _a:_a + 32].bitcast(BF16)
        _a = _take(128); ohblk = AFC.ap()[:, _a:_a + 128].bitcast(BF16)
        _a = _take(1); rowtotb = AFC.ap()[:, _a:_a + 1].bitcast(BF16)[:, 0:1]
        cnte = AFC.ap()[0:1, 6 * TBLF:6 * TBLF + E * NT]
        cntp = AFC.ap()[0:1, 7 * TBLF:7 * TBLF + E * NT]
        _a = _take(max(CAP // 128, 2)); gcol = AFC.ap()[:, _a:_a + max(CAP // 128, 2)]
        _a = _take(E); ca8 = AFC.ap()[0:NCORE, _a:_a + E]
        _a = _take(1); ltm8 = AFC.ap()[0:NCORE, _a:_a + 1]
        _a = _take(E); offs8 = AFC.ap()[0:1, _a:_a + E]
        _a = _take(NT); rowe = AFC.ap()[0:1, _a:_a + NT]
        _a = _take(NT); le_sb = AFC.ap()[:, _a:_a + NT]
        assert _cf <= max(H * M, 11 * TBLF + 4 * NT + 1024), _cf

        identb = sbuf("identb", [128, 128], BF16).ap()
        identf = sbuf("identf", [128, 128], F32).ap()
        ones128b = sbuf("ones128b", [128, 1], BF16).ap()
        onescolf = sbuf("onescolf", [128, 1], F32).ap()
        onesrowb = sbuf("onesrowb", [1, 512], BF16).ap()
        onesrowf = sbuf("onesrowf", [1, 128], F32).ap()
        ones2b = sbuf("ones2b", [128, 2], BF16).ap()
        Lb = sbuf("Lb", [128, 128], BF16).ap()
        ptbd = sbuf("ptbd", [128, 2 * M], BF16).ap()
        cidrep = sbuf("cidrep", [128, 1], F32).ap()
        e1000 = sbuf("e1000", [128, E], F32).ap()
        hiio = sbuf("hiio", [128, 64], F32).ap()
        piota = sbuf("piota", [128, 1], F32).ap()
        plo1 = sbuf("plo1", [128, 1], F32).ap()
        wrhl = sbuf("wrhl", [128, 2 * E * DC], BF16).ap()
        wrhi = sbuf("wrhi", [128, E * DC], BF16).ap()
        wrf = sbuf("wrf", [128, E * DC], F32).ap()
        cid_sb = sbuf("cid_sb", [1, 1], F32).ap()
        kconst = sbuf("kconst", [128, 2], F32).ap()
        dfTi = sbuf("dfTi", [64, 64], I32).ap()
        idx2i = sbuf("idx2i", [128, TC // 128], I32).ap()
        logbuf = AB3.ap()[:, 8448:8448 + 2 * NT * E].bitcast(F32)
        pload = sbuf("pload", [128, DH], BF16).ap()
        maskbig = sbuf("maskbig", [128, E * NT], BF16).ap()
        maskf = sbuf("maskf", [128, E * NT], F32).ap()
        eidx_mine = sbuf("eidx_mine", [128, NT], F32).ap()
        ptok_mine = sbuf("ptok_mine", [128, NT], F32).ap()
        lmax = sbuf("lmax", [128, NT], F32).ap()
        negl = sbuf("negl", [128, NT], F32).ap()
        sume = sbuf("sume", [128, NT], F32).ap()
        lg8 = sbuf("lg8", [128, E], F32).ap()
        mk8 = sbuf("mk8", [128, E], F32).ap()
        tmp8 = sbuf("tmp8", [128, E], F32).ap()
        srow = sbuf("srow", [128, 1], F32).ap()
        ssq = sbuf("ssq", [128, 1], F32).ap()
        mrow = sbuf("mrow", [128, 1], F32).ap()
        m2row = sbuf("m2row", [128, 1], F32).ap()
        vrow = sbuf("vrow", [128, 1], F32).ap()
        sqv = sbuf("sqv", [128, 1], F32).ap()
        rstd = sbuf("rstd", [128, 1], F32).ap()
        xnsumT = sbuf("xnsumT", [128, DC], F32).ap()
        xnsumb = sbuf("xnsumb", [128, DC], BF16).ap()
        negdT = sbuf("negdT", [128, 8], F32).ap()
        mxbuf = sbuf("mxbuf", [128, max(NTB * NP * 4, 128)], F32).ap()
        mxcol = sbuf("mxcol", [128, 1], F32).ap()
        mxone = sbuf("mxone", [1, 1], F32).ap()
        vsum_sb = sbuf("vsum_sb", [DH + 1, H], F32).ap()
        counts8 = sbuf("counts8", [1, E], F32).ap()
        eidxT = sbuf("eidxT", [NT, 128], F32).ap()
        ptokT = sbuf("ptokT", [NT, 128], F32).ap()
        mxall = sbuf("mxall", [NCORE, 1], F32).ap()
        mxsc = sbuf("mxsc", [1, 1], F32).ap()
        msca = sbuf("msca", [1, 1], F32).ap()
        mrep = sbuf("mrep", [128, 1], F32).ap()
        epsv = sbuf("epsv", [DH + 1, H], F32).ap()
        cols_sb = sbuf("cols_sb", [DH + 1, H], F32).ap()
        q2b = sbuf("q2b", [128, 512], BF16).ap()
        negdq = sbuf("negdq", [128, 2 * NT], F32).ap()
        rmx2 = sbuf("rmx2", [128, 2], F32).ap()
        biasq = sbuf("biasq", [128, 2], F32).ap()
        dinv2 = sbuf("dinv2", [128, 2], F32).ap()

        bc_reg = [None]

        def s0_g(g):
            bc_reg[0] = g.to_reg(T - 1)
            g.memset(identb[:], 0.0)
            g.memset(identf[:], 0.0)
            g.memset(Lb[:], 0.0)
            g.memset(ones128b[:], 1.0)
            g.memset(onescolf[:], 1.0)
            g.memset(onesrowb[:], 1.0)
            g.memset(onesrowf[:], 1.0)
            g.memset(ones2b[0:64, 0:1], 1.0)
            g.memset(ones2b[64:128, 0:1], 0.0)
            g.memset(ones2b[0:64, 1:2], 0.0)
            sGE.inc(g.memset(ones2b[64:128, 1:2], 1.0))
            sGE.wait(g)
            g.affine_select(out=identb[:], in_=identb[:], compare_op=ALU.not_equal,
                            fill=1.0, base=0, pattern=[[-1, 128]], channel_multiplier=1)
            g.affine_select(out=identf[:], in_=identf[:], compare_op=ALU.not_equal,
                            fill=1.0, base=0, pattern=[[-1, 128]], channel_multiplier=1)
            g.affine_select(out=Lb[:], in_=Lb[:], compare_op=ALU.is_ge,
                            fill=1.0, base=0, pattern=[[-1, 128]], channel_multiplier=1)
            g.memset(kconst[:, 0:1], LN_EPS)
            g.memset(kconst[:, 1:2], LNR)
            g.iota(e1000[:], pattern=[[1, E]], base=1000, channel_multiplier=0,
                   allow_small_or_imprecise_dtypes=True)
            g.iota(hiio[:], pattern=[[1, 64]], base=0, channel_multiplier=0,
                   allow_small_or_imprecise_dtypes=True)
            g.iota(piota[:], pattern=[[0, 1]], base=0, channel_multiplier=1,
                   allow_small_or_imprecise_dtypes=True)
            sGE.inc(g.iota(plo1[:], pattern=[[0, 1]], base=128, channel_multiplier=1,
                           allow_small_or_imprecise_dtypes=True))
            sWL.inc(g.dma_start(out=wrhi.rearrange("p (c e) -> p c e", e=E),
                                in_=wr_d.rearrange("(c p) e -> p c e", p=128)), 16)
            sWL.inc(g.dma_start(out=pload[:], in_=p_d[0:128, :]), 16)
        on("gpsimd", s0_g)
        GE_CONST = sGE.n

        def s0_s(s):
            sST.inc(s.dma_start(out=bo_row[:], in_=bo_d[:, :]), 16)
            sST.inc(s.dma_start(out=cid_sb[:], in_=cid_d[:, :]), 16)
            sST.inc(s.dma_start(out=wrf.rearrange("p (c e) -> p c e", e=E),
                                in_=wr_d.rearrange("(c p) e -> p c e", p=128)), 16)
        on("sync", s0_s)

        def s0_t1(t):
            t.wait_ge(sGE.h, GE_CONST)
            sWL.wait(t)
            t.transpose(bkb[6][0:64, 0:128], pload[:], identb[:])
            sBKP[6].inc(t.last)
        on("tensor", s0_t1)

        def s0_v1(v):
            sBKP[6].wait(v)
            v.memset(ptbd[:], 0.0)
            v.drain()
            v.tensor_scalar(ptbd[0:64, 0:128], bkb[6][0:64, 0:128], DN, None, ALU.mult)
            sBKC[6].inc(v.tensor_scalar(ptbd[64:128, M:M + 128], bkb[6][0:64, 0:128],
                                        DN, None, ALU.mult))
            sVE.inc(v.drain())
        on("vector", s0_v1)

        def s0_g2(g):
            sVE.wait(g)
            sWL.inc(g.dma_start(out=pload[:], in_=p_d[128:256, :]), 16)
        on("gpsimd", s0_g2)

        def s0_t2(t):
            sWL.wait(t)
            bfree(t, 6)
            t.transpose(bkb[6][0:64, 256:384], pload[:], identb[:])
            sBKP[6].inc(t.last)
            sST.wait(t)
            bfree(t, 7)
            t.matmul(bank[7][:, 0:1], onesrowf[:, :], cid_sb[:, :], start=True, stop=True)
            sBKP[7].inc(t.last)
        on("tensor", s0_t2)

        def s0_v2(v):
            sBKP[6].wait(v)
            sBKP[7].wait(v)
            v.tensor_scalar(ptbd[0:64, 128:256], bkb[6][0:64, 256:384], DN, None, ALU.mult)
            sBKC[6].inc(v.tensor_scalar(ptbd[64:128, M + 128:2 * M], bkb[6][0:64, 256:384],
                                        DN, None, ALU.mult))
            sBKC[7].inc(v.tensor_copy(cidrep[:], bank[7][:, 0:1]))
            v.drain()
            sWL.wait(v)
            for c in range(DC):
                v.tensor_copy(wrhl[:, c * 16:c * 16 + 8], wrhi[:, c * 8:(c + 1) * 8])
            v.drain()
            for c in range(DC):
                v.tensor_tensor(wrhl[:, c * 16 + 8:c * 16 + 16],
                                wrf[:, c * 8:(c + 1) * 8],
                                wrhl[:, c * 16:c * 16 + 8], ALU.subtract)
            sVE.inc(v.drain())
        on("vector", s0_v2)
        VE_CONST = sVE.n

        p0_xn_ve = [0] * NT
        p0_evac_ve = [0] * NT
        p0_rt_mark = [None] * NT
        p0_st_n = [0] * NT

        for tc in range(NT):
            par = tc % 2

            def p0_load(s, tc=tc, par=par):
                if tc >= 2:
                    s.wait_ge(sVE.h, p0_xn_ve[tc - 2])
                sXL[par].inc(s.dma_start(out=xbuf[par],
                                         in_=x_d[tc * 128:(tc + 1) * 128, :]), 16)
            on("sync", p0_load)

            def p0_v1(v, tc=tc, par=par):
                sXL[par].wait(v)
                if tc == 0:
                    v.wait_ge(sVE.h, VE_CONST)
                v.drain()
                sVE.inc(v.tensor_reduce(srow[:], xbuf[par], axis=AX.X, op=ALU.add))
            on("vector", p0_v1)

            def p0_a1(sc, tc=tc, par=par):
                sXL[par].wait(sc)
                if tc >= 1:
                    sc.wait_ge(sVE.h, p0_xn_ve[tc - 1])
                sc.drain()
                sAE.inc(sc.activation(out=xnfp, in_=xbuf[par], func=AF.Square,
                                      bias=0.0, scale=1.0, accum_out=ssq[:]))
            on("scalar", p0_a1)

            def p0_v2(v, tc=tc):
                sAE.wait(v)
                sVE.wait(v)
                v.drain()
                v.tensor_scalar(mrow[:], srow[:], 1.0 / D, None, ALU.mult)
                v.drain()
                v.tensor_tensor(m2row[:], mrow[:], mrow[:], ALU.mult)
                v.tensor_scalar(vrow[:], ssq[:], 1.0 / D, None, ALU.mult)
                v.drain()
                sVE.inc(v.tensor_tensor(vrow[:], vrow[:], m2row[:], ALU.subtract))
            on("vector", p0_v2)

            def p0_a2(sc, tc=tc):
                sVE.wait(sc)
                sc.drain()
                sAE.inc(sc.activation(out=sqv[:], in_=vrow[:], func=AF.Sqrt,
                                      bias=kconst[:, 0:1], scale=1.0))
            on("scalar", p0_a2)

            def p0_v3(v, tc=tc, par=par):
                sAE.wait(v)
                if tc >= 2:
                    v.wait_ge(sXN.h, p0_st_n[tc - 1])
                    brt2, bkn2 = p0_rt_mark[tc - 2]
                    v.wait_ge(sBKP[brt2].h, bkn2)
                v.drain()
                v.reciprocal(rstd[:], sqv[:])
                v.drain()
                v.tensor_scalar(xnfp, xbuf[par], mrow[:], None, ALU.subtract)
                v.drain()
                v.tensor_scalar(xnfp, xnfp, rstd[:], None, ALU.mult)
                v.drain()
                v.tensor_copy(xnbf[par], xnfp)
                v.drain()
                sVE.inc(v.tensor_tensor(xloT[par], xnfp, xnbf[par], ALU.subtract))
                p0_xn_ve[tc] = sVE.n
            on("vector", p0_v3)

            def p0_st(s, tc=tc, par=par):
                s.wait_ge(sVE.h, p0_xn_ve[tc])
                sXN.inc(s.dma_start(out=xn_dram[tc * 128:(tc + 1) * 128, :],
                                    in_=xnbf[par]), 16)
                p0_st_n[tc] = sXN.n
            on("sync", p0_st)

            def p0_tr(t, tc=tc, par=par):
                t.wait_ge(sVE.h, p0_xn_ve[tc])
                bfree(t, 4)
                for c in range(DC):
                    t.transpose(bkb[4][:, c * 128:(c + 1) * 128],
                                xnbf[par][:, c * 128:(c + 1) * 128], identb[:])
                sBKP[4].inc(t.last)
                bfree(t, 6)
                for c in range(DC):
                    t.transpose(bkb[6][:, c * 128:(c + 1) * 128],
                                xloT[par][:, c * 128:(c + 1) * 128], identb[:])
                sBKP[6].inc(t.last)
            on("tensor", p0_tr)

            def p0_ev(v, tc=tc, par=par):
                sBKP[4].wait(v)
                sBKP[6].wait(v)
                v.drain()
                for c in range(DC):
                    v.tensor_copy(xnT[:, c * TC + tc * 128: c * TC + (tc + 1) * 128],
                                  bkb[4][:, c * 128:(c + 1) * 128])
                sBKC[4].inc(v.drain())
                for c in range(DC):
                    v.tensor_copy(xloT[par][:, c * 128:(c + 1) * 128],
                                  bkb[6][:, c * 128:(c + 1) * 128])
                sBKC[6].inc(v.drain())
                sVE.inc(v.drain())
                p0_evac_ve[tc] = sVE.n
            on("vector", p0_ev)

            brt = 5 if tc % 2 == 0 else 7
            bra = 3 if tc % 2 == 0 else 2

            def p0_rt(t, tc=tc, par=par, brt=brt, bra=bra):
                t.wait_ge(sVE.h, p0_evac_ve[tc])
                bfree(t, brt)
                bfree(t, bra)
                for c in range(DC):
                    t.matmul(bank[brt][:, 0:16],
                             xnT[:, c * TC + tc * 128: c * TC + (tc + 1) * 128],
                             wrhl[:, c * 16:(c + 1) * 16],
                             start=(c == 0), stop=(c == DC - 1))
                sBKP[brt].inc(t.last)
                for c in range(DC):
                    t.matmul(bank[bra][:, 32:40],
                             xloT[par][:, c * 128:(c + 1) * 128],
                             wrhi[:, c * 8:(c + 1) * 8],
                             start=(c == 0), stop=(c == DC - 1))
                sBKP[bra].inc(t.last)
                p0_rt_mark[tc] = (bra, sBKP[bra].n)
            on("tensor", p0_rt)

            def p0_rl(v, tc=tc, brt=brt, bra=bra):
                sBKP[brt].wait(v)
                sBKP[bra].wait(v)
                v.drain()
                v.tensor_copy(lg8[:], bank[brt][:, 0:8])
                v.drain()
                sBKC[brt].inc(v.tensor_tensor(lg8[:], lg8[:], bank[brt][:, 8:16], ALU.add))
                v.drain()
                sBKC[bra].inc(v.tensor_tensor(logbuf[:, tc * E:(tc + 1) * E],
                                              lg8[:], bank[bra][:, 32:40], ALU.add))
                v.drain()
                v.tensor_reduce(lmax[:, tc:tc + 1], logbuf[:, tc * E:(tc + 1) * E],
                                axis=AX.X, op=ALU.max)
                v.drain()
                v.tensor_scalar(mk8[:], logbuf[:, tc * E:(tc + 1) * E],
                                lmax[:, tc:tc + 1], None, ALU.is_ge)
                v.drain()
                v.scalar_tensor_tensor(tmp8[:], mk8[:], -1000.0, e1000[:],
                                       ALU.mult, ALU.add)
                v.drain()
                sVE.inc(v.tensor_reduce(eidx_mine[:, tc:tc + 1], tmp8[:],
                                        axis=AX.X, op=ALU.min))
            on("vector", p0_rl)

        def p0_fv(v):
            v.drain()
            sVE.inc(v.tensor_scalar(negl[:], lmax[:], -1.0, None, ALU.mult))
        on("vector", p0_fv)
        NEGL_VE = sVE.n

        def p0_fa(sc):
            sc.wait_ge(sVE.h, NEGL_VE)
            sc.drain()
            for tc in range(NT):
                sc.activation(out=logbuf[:, tc * E:(tc + 1) * E],
                              in_=logbuf[:, tc * E:(tc + 1) * E], func=AF.Exp,
                              bias=negl[:, tc:tc + 1], scale=1.0,
                              accum_out=sume[:, tc:tc + 1])
                sc.drain()
            sAE.inc(sc.drain())
        on("scalar", p0_fa)

        def p0_fv2(v):
            sAE.wait(v)
            v.drain()
            v.reciprocal(ptok_mine[:], sume[:])
            v.drain()
            for e in range(E):
                v.tensor_scalar(maskf[:, e * NT:(e + 1) * NT], eidx_mine[:],
                                float(e), None, ALU.is_equal)
            v.drain()
            sVE.inc(v.tensor_copy(maskbig[:], maskf[:]))
        on("vector", p0_fv2)
        P0_VE = sVE.n

        if debug >= 1:
            def dbg_p0(s):
                s.wait_ge(sVE.h, P0_VE)
                sDBG.inc(s.dma_start(out=dbg["eidx"][:, :], in_=eidx_mine[:]), 16)
                sDBG.inc(s.dma_start(out=dbg["ptok"][:, :], in_=ptok_mine[:]), 16)
            on("sync", dbg_p0)

            def dbg_p0b(g):
                g.wait_ge(sXN.h, sXN.n)
                sDBG2.inc(g.dma_start(out=dbg["xn"][:, :], in_=xn_dram[:, :]), 16)
            on("gpsimd", dbg_p0b)


        def _stub_final():
            def zv(v):
                v.drain()
                sVE.inc(v.memset(y1buf[0], 0.0))
            on("vector", zv)
            zve = sVE.n

            def zs(s):
                s.wait_ge(sVE.h, zve)
                for tc2 in range(NT):
                    sOUT.inc(s.dma_start(out=y_d[tc2 * 128:(tc2 + 1) * 128, :],
                                         in_=y1buf[0]), 16)
                s.wait_ge(sOUT.h, sOUT.n)
            on("sync", zs)

        def ag_xn(g):
            g.wait_ge(sXN.h, sXN.n)
            sCC1.inc(g.collective_compute(
                "AllGather", ALU.bypass, replica_groups=[list(range(NCORE))],
                ins=[xn_dram.opt()], outs=[xnall.opt()]))
        on("gpsimd", ag_xn)
        if trunc == 1:
            _stub_final()
            return nc

        def a_wld(g):
            sWL.inc(g.dma_start(out=wv_sb.rearrange("p (c j) -> p c j", j=D),
                                in_=wv_d.rearrange("(c p) j -> p c j", p=128)), 16)
            sWL.inc(g.dma_start(out=wkfull.rearrange("p (c j) -> p c j", j=D),
                                in_=wk_d.rearrange("(c p) j -> p c j", p=128)), 16)
        on("gpsimd", a_wld)
        WL_A = sWL.n

        def a_vpo(g):
            for i in range(2):
                g.memset(vp[i][:, 64::65], 1.0)
            sGE.inc(g.last)
        on("gpsimd", a_vpo)
        GE_VPO = sGE.n

        kp_te_free = [0, 0]
        ctx_mark = [0]
        vp_te_mark = [0, 0]
        kev_gate_te = [None, None]
        ksq_gate_te = None
        dg_gate_ae = 0

        for tb in range(NTB):
            par = tb % 2

            for sub in range(4):
                tcg = tb * 4 + sub

                def a_vmm(t, tb=tb, sub=sub, tcg=tcg, par=par):
                    if tb == 0 and sub == 0:
                        t.wait_ge(sWL.h, WL_A)
                        t.wait_ge(sGE.h, GE_VPO)
                    t.wait_ge(sVE.h, p0_evac_ve[tcg])
                    if tb >= 2 and sub == 0:
                        t.wait_ge(sTE.h, vp_te_mark[par])
                    bfree(t, 0)
                    bfree(t, 1)
                    for c in range(DC):
                        t.matmul(bank[0][:, :],
                                 xnT[:, c * TC + tcg * 128: c * TC + (tcg + 1) * 128],
                                 wv_sb[:, c * D: c * D + 512],
                                 start=(c == 0), stop=(c == DC - 1))
                    sBKP[0].inc(t.last)
                    for c in range(DC):
                        t.matmul(bank[1][:, :],
                                 xnT[:, c * TC + tcg * 128: c * TC + (tcg + 1) * 128],
                                 wv_sb[:, c * D + 512: c * D + 1024],
                                 start=(c == 0), stop=(c == DC - 1))
                    sBKP[1].inc(t.last)
                on("tensor", a_vmm)

                def a_vev(v, sub=sub, par=par):
                    sBKP[0].wait(v)
                    sBKP[1].wait(v)
                    v.drain()
                    for h in range(H):
                        src = bank[0] if h < 8 else bank[1]
                        off = (h % 8) * 64
                        v.tensor_copy(vp[par][:, (sub * H + h) * 65:(sub * H + h) * 65 + 64],
                                      src[:, off:off + 64])
                    sBKC[0].inc(v.drain())
                    sBKC[1].inc(v.drain())
                    sVE.inc(v.drain())
                on("vector", a_vev)
            vp_ready = sVE.n

            for pair in range(NP):
                if trunc == 11:
                    continue
                kpar = pair % 2
                bc0, bc1 = (4, 6) if pair % 2 == 0 else (5, 7)
                bcap0 = bkf[4] if pair % 2 == 0 else bank[5]
                bcap1 = bkf[6] if pair % 2 == 0 else bank[7]

                def a_kmm(t, tb=tb, pair=pair):
                    bfree(t, 2)
                    for c in range(DC):
                        t.matmul(bank[2][:, :],
                                 wkfull[:, c * D + pair * 128: c * D + (pair + 1) * 128],
                                 xnT[:, c * TC + tb * 512: c * TC + (tb + 1) * 512],
                                 start=(c == 0), stop=(c == DC - 1))
                    sBKP[2].inc(t.last)
                on("tensor", a_kmm)

                if True:
                    def a_kev(v, kpar=kpar, pair=pair):
                        sBKP[2].wait(v)
                        if kev_gate_te[kpar]:
                            v.wait_ge(*kev_gate_te[kpar])
                        v.drain()
                        sVE.inc(v.tensor_copy(kblk[kpar], bank[2][:, :]))
                    on("vector", a_kev)
                kev_ve = sVE.n

                if True:
                    def a_ksq(sc, tb=tb, pair=pair, kpar=kpar, kev_ve=kev_ve):
                        sc.wait_ge(sVE.h, kev_ve)
                        if ksq_gate_te:
                            sc.wait_ge(*ksq_gate_te)
                        sc.drain()
                        sAE.inc(sc.activation(out=k2b[:], in_=kblk[kpar],
                                              func=AF.Square, bias=0.0, scale=DN))
                    on("scalar", a_ksq)
                ksq_ae = sAE.n

                if trunc in (121, 1211, 1212, 1213):
                    def a_rel(t, ksq_ae=ksq_ae, kev_ve=kev_ve):
                        sAE.wait(t, ksq_ae)
                        t.wait_ge(sVE.h, kev_ve)
                        sBKC[2].inc(t.nop())
                    on("tensor", a_rel)
                    continue

                def a_dgm(t, ksq_ae=ksq_ae, kev_ve=kev_ve):
                    sAE.wait(t, ksq_ae)
                    t.wait_ge(sVE.h, kev_ve)
                    sBKC[2].inc(t.nop())
                    bfree(t, 3)
                    t.matmul(bank[3][0:2, 0:512], ones2b[:, :], k2b[:], start=True, stop=True)
                    sBKP[3].inc(t.last)
                on("tensor", a_dgm)
                ksq_gate_te = (sBKP[3].h, sBKP[3].n)

                def a_dge(v, dg_gate_ae=dg_gate_ae):
                    sBKP[3].wait(v)
                    if dg_gate_ae:
                        v.wait_ge(sAE.h, dg_gate_ae)
                    v.drain()
                    sBKC[3].inc(v.tensor_scalar(diagp[:], bank[3][0:2, 0:512],
                                                0.5, None, ALU.mult))
                    sVE.inc(v.drain())
                on("vector", a_dge)
                dge_ve = sVE.n

                def a_dgt(t, dge_ve=dge_ve):
                    t.wait_ge(sVE.h, dge_ve)
                    bfree(t, 3)
                    for sub in range(4):
                        t.transpose(bank[3][:, sub * 2: sub * 2 + 2],
                                    diagp[:, sub * 128:(sub + 1) * 128], identf[0:2, 0:2])
                    sBKP[3].inc(t.last)
                on("tensor", a_dgt)

                def a_dg2(v):
                    sBKP[3].wait(v)
                    v.drain()
                    sBKC[3].inc(v.tensor_scalar(negdT[:], bank[3][:, 0:8],
                                                -1.0, None, ALU.mult))
                    sVE.inc(v.drain())
                on("vector", a_dg2)
                negd_ve = sVE.n

                if trunc == 12:
                    def a_rel2(v):
                        sVE.inc(v.drain())
                    on("vector", a_rel2)
                    continue
                for sub in range(4):
                    bdd = 3 if sub % 2 == 0 else 2
                    kpp = sub % 2

                    def a_dd(t, sub=sub, bdd=bdd, kpar=kpar, kev_ve=kev_ve):
                        t.wait_ge(sVE.h, kev_ve)
                        bfree(t, bdd)
                        t.matmul(bank[bdd][:, 0:512],
                                 kblk[kpar][:, sub * 128:(sub + 1) * 128],
                                 ptbd[:, :], start=True, stop=True)
                        sBKP[bdd].inc(t.last)
                    on("tensor", a_dd)

                    if trunc in (131, 133):
                        def a_mxstub(v, bdd=bdd):
                            sBKP[bdd].wait(v)
                            sBKC[bdd].inc(v.drain())
                            sVE.inc(v.drain())
                        on("vector", a_mxstub)
                        mx_ve = sVE.n
                    else:
                        pass

                    def a_mx(v, tb=tb, pair=pair, sub=sub, bdd=bdd):
                        sBKP[bdd].wait(v)
                        col = (tb * NP + pair) * 4 + sub
                        v.drain()
                        sBKC[bdd].inc(v.tensor_reduce(mxbuf[:, col:col + 1],
                                                      bank[bdd][:, :], axis=AX.X, op=ALU.max))
                        sVE.inc(v.drain())
                    if trunc not in (131, 133):
                        on("vector", a_mx)
                        mx_ve = sVE.n

                    if trunc in (131, 132):
                        def a_exstub(sc, bdd=bdd, mx_ve=mx_ve, kpp=kpp):
                            sc.wait_ge(sVE.h, mx_ve)
                            sc.wait_ge(sTE.h, kp_te_free[kpp])
                            sc.drain()
                            sBKC[bdd].inc(sc.drain())
                            sAE.inc(sc.drain())
                        on("scalar", a_exstub)
                        exp_ae = sAE.n
                        if sub == 3:
                            dg_gate_ae = exp_ae

                    def a_ex(sc, sub=sub, bdd=bdd, kpp=kpp, negd_ve=negd_ve,
                             mx_ve=mx_ve):
                        sBKP[bdd].wait(sc)
                        sc.wait_ge(sVE.h, mx_ve)
                        sc.wait_ge(sVE.h, negd_ve)
                        sc.wait_ge(sTE.h, kp_te_free[kpp])
                        sc.drain()
                        sc.activation(out=kp_sb[kpp][:, 0:256], in_=bank[bdd][:, 0:256],
                                      func=AF.Exp, bias=negdT[:, sub * 2: sub * 2 + 1],
                                      scale=1.0)
                        sc.drain()
                        sBKC[bdd].inc(sc.activation(
                            out=kp_sb[kpp][:, 256:512], in_=bank[bdd][:, 256:512],
                            func=AF.Exp, bias=negdT[:, sub * 2 + 1: sub * 2 + 2],
                            scale=1.0))
                        sAE.inc(sc.drain())
                    if trunc not in (131, 132):
                        on("scalar", a_ex)
                        exp_ae = sAE.n
                        if sub == 3:
                            dg_gate_ae = exp_ae

                    if trunc in (13, 131, 132, 133):
                        kp_te_free[kpp] = sTE.n
                        if sub == 3:
                            ctx_mark[0] = sTE.n
                        continue

                    def a_m3(t, tb=tb, pair=pair, sub=sub, par=par, kpp=kpp,
                             exp_ae=exp_ae, bc0=bc0, bc1=bc1, bcap0=bcap0,
                             bcap1=bcap1, vp_ready=vp_ready):
                        sAE.wait(t, exp_ae)
                        t.wait_ge(sVE.h, vp_ready)
                        if sub == 0:
                            bfree(t, bc0)
                            bfree(t, bc1)
                        for h2 in range(2):
                            h = pair * 2 + h2
                            cap = bcap0 if h2 == 0 else bcap1
                            t.matmul(cap[0:DH + 1, 0:256],
                                     vp[par][:, (sub * H + h) * 65:(sub * H + h + 1) * 65],
                                     kp_sb[kpp][:, h2 * 256:(h2 + 1) * 256],
                                     start=(sub == 0), stop=(sub == 3))
                        sTE.inc(t.last)
                        kp_te_free[kpp] = sTE.n
                        if sub == 3:
                            ctx_mark[0] = sTE.n
                    on("tensor", a_m3)

                kev_gate_te[kpar] = (sBKP[2].h, sBKP[2].n)

                if trunc in (13, 131, 132, 133):
                    continue
                cm = ctx_mark[0]

                def a_ctx(v, tb=tb, pair=pair, bc0=bc0, bc1=bc1, bcap0=bcap0,
                          bcap1=bcap1, cm=cm):
                    v.wait_ge(sTE.h, cm)
                    v.drain()
                    for h2 in range(2):
                        cap = bcap0 if h2 == 0 else bcap1
                        bci = bc0 if h2 == 0 else bc1
                        dst = ctx_acc[:, (pair * 2 + h2) * 256:(pair * 2 + h2 + 1) * 256]
                        if tb == 0:
                            sBKC[bci].inc(v.tensor_copy(dst, cap[0:DH + 1, 0:256]))
                        else:
                            sBKC[bci].inc(v.tensor_tensor(dst, dst,
                                                          cap[0:DH + 1, 0:256], ALU.add))
                        v.drain()
                    sVE.inc(v.drain())
                on("vector", a_ctx)

            vp_te_mark[par] = sTE.n
        def a_bar(t):
            sTE.inc(t.drain())
        on("tensor", a_bar)
        A_BAR = sTE.n
        if trunc in (11, 12, 121, 1211, 1212, 1213, 13, 131, 132, 133, 14):
            _stub_final()
            return nc
        A_VE = sVE.n

        def a_mxf(v):
            v.wait_ge(sVE.h, A_VE)
            v.drain()
            sVE.inc(v.tensor_reduce(mxcol[:], mxbuf[:, 0:NTB * NP * 4],
                                    axis=AX.X, op=ALU.max))
        on("vector", a_mxf)
        MXC_VE = sVE.n

        def a_mxt(t):
            t.wait_ge(sVE.h, MXC_VE)
            bfree(t, 3)
            t.transpose(bank[3][0:1, 16:144], mxcol[:], identf[:])
            sBKP[3].inc(t.last)
        on("tensor", a_mxt)

        def a_mxv(v):
            sBKP[3].wait(v)
            v.drain()
            sBKC[3].inc(v.tensor_reduce(mxone[:], bank[3][0:1, 16:144],
                                        axis=AX.X, op=ALU.max))
            sVE.inc(v.drain())
        on("vector", a_mxv)
        MXONE_VE = sVE.n

        def a_xns(v):
            v.drain()
            for c in range(DC):
                v.tensor_reduce(xnsumT[:, c:c + 1], xnT[:, c * TC:(c + 1) * TC],
                                axis=AX.X, op=ALU.add)
            v.drain()
            sVE.inc(v.tensor_copy(xnsumb[:], xnsumT[:]))
        on("vector", a_xns)
        XNS_VE = sVE.n

        def a_vsm(t):
            t.wait_ge(sVE.h, XNS_VE)
            bfree(t, 3)
            for vcc in range(DC):
                for c in range(DC):
                    t.matmul(bank[3][:, 32 + vcc: 32 + vcc + 1],
                             wv_sb[:, c * D + vcc * 128: c * D + vcc * 128 + 128],
                             xnsumb[:, c:c + 1],
                             start=(c == 0), stop=(c == DC - 1))
            sBKP[3].inc(t.last)
        on("tensor", a_vsm)
        A_TE = (sBKP[3].h, sBKP[3].n)

        def a_vse(v):
            sBKP[3].wait(v)
            v.drain()
            for h in range(H):
                vcc = h // 2
                base = (h % 2) * 64
                v.tensor_copy(vsum_sb[0:64, h:h + 1],
                              bank[3][base:base + 64, 32 + vcc:32 + vcc + 1])
            sBKC[3].inc(v.drain())
            v.memset(vsum_sb[64:65, :], float(TC))
            sVE.inc(v.drain())
        on("vector", a_vse)
        VSUM_VE = sVE.n

        def b_cm(t):
            t.wait_ge(sVE.h, P0_VE)
            bfree(t, 5)
            t.matmul(bank[5][0:1, 0:E * NT], ones128b[:, :], maskbig[:], start=True, stop=True)
            sBKP[5].inc(t.last)
            bfree(t, 7)
            t.transpose(bank[7][0:NT, 0:128], eidx_mine[:], identf[:])
            t.transpose(bank[7][0:NT, 128:256], ptok_mine[:], identf[:])
            sBKP[7].inc(t.last)
        on("tensor", b_cm)

        def b_ce(v):
            sBKP[5].wait(v)
            sBKP[7].wait(v)
            v.drain()
            sBKC[5].inc(v.tensor_reduce(
                counts8[:],
                bank[5][0:1, 0:E * NT].rearrange("a (e t) -> a e t", t=NT),
                axis=AX.X, op=ALU.add))
            v.tensor_copy(eidxT[:], bank[7][0:NT, 0:128])
            sBKC[7].inc(v.tensor_copy(ptokT[:], bank[7][0:NT, 128:256]))
            sVE.inc(v.drain())
        on("vector", b_ce)
        MSGB_VE = sVE.n

        def msg_st(s):
            s.wait_ge(sVE.h, A_VE)
            sMSG.inc(s.dma_start(out=msgA_in[:, 0:H * M], in_=ctx_acc[:, :]), 16)
            s.wait_ge(sVE.h, VSUM_VE)
            sMSG.inc(s.dma_start(out=msgA_in[:, H * M:], in_=vsum_sb[:, :]), 16)
            s.wait_ge(sVE.h, MXONE_VE)
            s.wait_ge(sVE.h, MSGB_VE)
            sMSG.inc(s.dma_start(out=msgB_in[:, 0:1], in_=mxone[:, :]), 16)
            sMSG.inc(s.dma_start(out=msgB_in[:, 1:1 + E], in_=counts8[:, :]), 16)
            sMSG.inc(s.dma_start(out=msgB_in[:, 1 + E:16], in_=e1000[0:1, 0:15 - E]), 16)
            sMSG.inc(s.dma_start(
                out=msgB_in[0:1, 16:16 + TC].rearrange("a (t p) -> (a t) p", p=128),
                in_=eidxT[:, :]), 16)
            sMSG.inc(s.dma_start(
                out=msgB_in[0:1, 16 + TC:16 + 2 * TC].rearrange("a (t p) -> (a t) p", p=128),
                in_=ptokT[:, :]), 16)
        on("sync", msg_st)

        def msg_cc(g):
            g.wait_ge(sMSG.h, sMSG.n)
            sCC2.inc(g.collective_compute(
                "AllReduce", ALU.add,
                replica_groups=[[0, 1], [2, 3], [4, 5], [6, 7]],
                ins=[msgA_in.opt()], outs=[msgA_out.opt()]))
            sCC3.inc(g.collective_compute(
                "AllGather", ALU.bypass, replica_groups=[list(range(NCORE))],
                ins=[msgB_in.opt()], outs=[msgB_out.opt()]))
        on("gpsimd", msg_cc)

        if trunc == 2:
            _stub_final()
            return nc
        if debug >= 2:
            def dbg_a(s):
                sCC2.wait(s)
                sDBG.inc(s.dma_start(out=dbg["ctx"][:, :], in_=msgA_out[:, :]), 16)
            on("sync", dbg_a)

        def cc_ld(s):
            sCC2.wait(s)
            sCC3.wait(s)
            sMGL.inc(s.dma_start(out=ctxF[:, 0:H * M], in_=msgA_out[:, 0:H * M]), 16)
            sMGL.inc(s.dma_start(out=vsum_sb[:, :], in_=msgA_out[:, H * M:]), 16)
            sMGL.inc(s.dma_start(out=mxall[:, :], in_=msgB_out[:, 0:1],
                                 allow_slow_non_contiguous=True), 16)
        on("sync", cc_ld)
        CCLD_M = sMGL.n

        def cc_t1(t):
            t.wait_ge(sMGL.h, CCLD_M)
            bfree(t, 7)
            t.transpose(bank[7][0:1, 0:NCORE], mxall[:, :], identf[0:NCORE, 0:NCORE])
            sBKP[7].inc(t.last)
        on("tensor", cc_t1)

        def cc_v1(v):
            sBKP[7].wait(v)
            v.drain()
            sBKC[7].inc(v.tensor_reduce(mxsc[:], bank[7][0:1, 0:NCORE],
                                        axis=AX.X, op=ALU.max))
            sVE.inc(v.drain())
        on("vector", cc_v1)
        MXS_VE = sVE.n

        if debug >= 2:
            def dbg_mx(s):
                s.wait_ge(sVE.h, MXS_VE)
                sDBG.inc(s.dma_start(out=dbg["mx"][:, :], in_=mxsc[:, :]), 16)
            on("sync", dbg_mx)

        def cc_a1(sc):
            sc.wait_ge(sVE.h, MXS_VE)
            sc.drain()
            sAE.inc(sc.activation(out=msca[:], in_=mxsc[:], func=AF.Exp,
                                  bias=kconst[0:1, 1:2], scale=-1.0))
        on("scalar", cc_a1)
        MSCA_AE = sAE.n

        def cc_t2(t):
            t.wait_ge(sAE.h, MSCA_AE)
            bfree(t, 7)
            t.matmul(bank[7][:, 20:21], onesrowf[:, :], msca[:, :], start=True, stop=True)
            sBKP[7].inc(t.last)
        on("tensor", cc_t2)

        def cc_v2(v):
            sBKP[7].wait(v)
            v.drain()
            sBKC[7].inc(v.tensor_copy(mrep[:], bank[7][:, 20:21]))
            v.drain()
            v.tensor_scalar(epsv[:, :], vsum_sb[:, :], RATIO * EPS_PERF, None, ALU.mult)
            v.drain()
            for h in range(H):
                v.tensor_scalar(ctxF[:, h * M:(h + 1) * M], ctxF[:, h * M:(h + 1) * M],
                                mrep[0:DH + 1, 0:1], None, ALU.mult)
            v.drain()
            for h in range(H):
                v.tensor_scalar(ctxF[:, h * M:(h + 1) * M], ctxF[:, h * M:(h + 1) * M],
                                epsv[:, h:h + 1], None, ALU.add)
            v.drain()
            for h in range(H):
                v.tensor_reduce(cols_sb[:, h:h + 1], ctxF[:, h * M:(h + 1) * M],
                                axis=AX.X, op=ALU.add)
            v.drain()
            sVE.inc(v.tensor_scalar(cols_sb[:, :], cols_sb[:, :], RATIO * EPS_PERF,
                                    None, ALU.mult))
        on("vector", cc_v2)
        CMB_VE = sVE.n

        for h in range(H):
            def cc_tr(t, h=h):
                if h == 0:
                    t.wait_ge(sVE.h, CMB_VE)
                bfree(t, 7)
                for mc in range(NMC):
                    t.transpose(bank[7][:, mc * 80: mc * 80 + 65],
                                ctxF[0:DH + 1, h * M + mc * 128: h * M + (mc + 1) * 128],
                                identf[0:DH + 1, 0:DH + 1])
                t.transpose(bank[7][0:1, 400:465], cols_sb[0:DH + 1, h:h + 1], identf[0:DH + 1, 0:DH + 1])
                sBKP[7].inc(t.last)
            on("tensor", cc_tr)

            def cc_te(v, h=h):
                sBKP[7].wait(v)
                v.drain()
                for mc in range(NMC):
                    v.tensor_copy(ctxT[:, (h * NMC + mc) * 65:(h * NMC + mc + 1) * 65],
                                  bank[7][:, mc * 80: mc * 80 + 65])
                sBKC[7].inc(v.tensor_copy(colrow[:, h * 65:(h + 1) * 65],
                                          bank[7][0:1, 400:465]))
                sVE.inc(v.drain())
            on("vector", cc_te)
        CTX_VE = sVE.n

        wq_te_free = [None, None]
        qp_te_free = [None, None]
        hp_te_free = [None]
        q2_gate_te = [None]
        dq_gate_ae = [0]
        m4s_prev_ae = [0]

        def b_wq01(g):
            g.wait_ge(*A_TE)
            for pair in range(2):
                sWQ[pair].inc(g.dma_start(
                    out=wq_sb[pair].rearrange("p (c j) -> p c j", j=128),
                    in_=wq_d[:, pair * 128:(pair + 1) * 128].rearrange(
                        "(c p) j -> p c j", p=128)), 16)
        on("gpsimd", b_wq01)

        for pair in range(NP):
            par = pair % 2

            if pair >= 2:
                def b_wq(g, pair=pair, par=par):
                    g.wait_ge(*wq_te_free[par])
                    sWQ[par].inc(g.dma_start(
                        out=wq_sb[par].rearrange("p (c j) -> p c j", j=128),
                        in_=wq_d[:, pair * 128:(pair + 1) * 128].rearrange(
                            "(c p) j -> p c j", p=128)), 16)
                on("gpsimd", b_wq)
            wq_wl = sWQ[par].n

            for ft in range(NTB):
                bq = ft % 2

                def b_qmm(t, pair=pair, par=par, ft=ft, bq=bq, wq_wl=wq_wl):
                    if ft == 0:
                        t.wait_ge(sWQ[par].h, wq_wl)
                        if pair == 0:
                            t.wait_ge(sVE.h, CTX_VE)
                    bfree(t, bq)
                    for c in range(DC):
                        t.matmul(bank[bq][:, :],
                                 wq_sb[par][:, c * 128:(c + 1) * 128],
                                 xnT[:, c * TC + ft * 512: c * TC + (ft + 1) * 512],
                                 start=(c == 0), stop=(c == DC - 1))
                    sBKP[bq].inc(t.last)
                on("tensor", b_qmm)

                def b_qev(v, ft=ft, bq=bq):
                    sBKP[bq].wait(v)
                    v.drain()
                    sVE.inc(v.tensor_copy(qblk[:, ft * 512:(ft + 1) * 512], bank[bq][:, :]))
                on("vector", b_qev)
                qev_ve = sVE.n

                def b_qsq(sc, ft=ft, bq=bq, qev_ve=qev_ve):
                    sc.wait_ge(sVE.h, qev_ve)
                    if q2_gate_te[0]:
                        sc.wait_ge(*q2_gate_te[0])
                    sc.drain()
                    sAE.inc(sc.activation(out=q2b[:], in_=qblk[:, ft * 512:(ft + 1) * 512],
                                          func=AF.Square, bias=0.0, scale=DN))
                on("scalar", b_qsq)
                qsq_ae = sAE.n

                def b_qdg(t, bq=bq, qsq_ae=qsq_ae, qev_ve=qev_ve):
                    sAE.wait(t, qsq_ae)
                    t.wait_ge(sVE.h, qev_ve)
                    sBKC[bq].inc(t.nop())
                    bfree(t, 3)
                    t.matmul(bank[3][0:2, 0:512], ones2b[:, :], q2b[:], start=True, stop=True)
                    sBKP[3].inc(t.last)
                    q2_gate_te[0] = (sBKP[3].h, sBKP[3].n)
                on("tensor", b_qdg)

                def b_qde(v):
                    sBKP[3].wait(v)
                    if dq_gate_ae[0]:
                        v.wait_ge(sAE.h, dq_gate_ae[0])
                    v.drain()
                    sBKC[3].inc(v.tensor_scalar(diagp[:], bank[3][0:2, 0:512],
                                                0.5, None, ALU.mult))
                    sVE.inc(v.drain())
                on("vector", b_qde)
                qde_ve = sVE.n

                def b_qdt(t, qde_ve=qde_ve):
                    t.wait_ge(sVE.h, qde_ve)
                    bfree(t, 3)
                    for sub in range(4):
                        t.transpose(bank[3][:, sub * 2: sub * 2 + 2],
                                    diagp[:, sub * 128:(sub + 1) * 128], identf[0:2, 0:2])
                    sBKP[3].inc(t.last)
                on("tensor", b_qdt)

                def b_qd2(v, ft=ft):
                    sBKP[3].wait(v)
                    v.drain()
                    sBKC[3].inc(v.tensor_scalar(negdq[:, ft * 8:(ft + 1) * 8],
                                                bank[3][:, 0:8], -1.0, None, ALU.mult))
                    sVE.inc(v.drain())
                on("vector", b_qd2)

            def b_ndl(v):
                v.drain()
                sVE.inc(v.tensor_scalar(negdq[:, :], negdq[:, :], 1.0, LNR,
                                        ALU.mult, ALU.add))
            on("vector", b_ndl)
            ndl_ve = sVE.n

            prev_exp_ae = [0]
            for tc in range(NT):
                bdd = 2 if tc % 2 == 0 else 3
                qpp = tc % 2

                def b_dq(t, tc=tc, bdd=bdd, ndl_ve=ndl_ve):
                    t.wait_ge(sVE.h, ndl_ve)
                    bfree(t, bdd)
                    t.matmul(bank[bdd][:, 0:512],
                             qblk[:, tc * 128:(tc + 1) * 128],
                             ptbd[:, :], start=True, stop=True)
                    sBKP[bdd].inc(t.last)
                on("tensor", b_dq)

                def b_rmx(v, tc=tc, bdd=bdd):
                    sBKP[bdd].wait(v)
                    if prev_exp_ae[0]:
                        v.wait_ge(sAE.h, prev_exp_ae[0])
                    v.drain()
                    v.tensor_reduce(rmx2[:, 0:1], bank[bdd][:, 0:256], axis=AX.X, op=ALU.max)
                    v.tensor_reduce(rmx2[:, 1:2], bank[bdd][:, 256:512], axis=AX.X, op=ALU.max)
                    v.drain()
                    sVE.inc(v.tensor_tensor(biasq[:, :], negdq[:, tc * 2:(tc + 1) * 2],
                                            rmx2[:, :], ALU.subtract))
                on("vector", b_rmx)
                rmx_ve = sVE.n

                def b_ex(sc, tc=tc, bdd=bdd, qpp=qpp, rmx_ve=rmx_ve):
                    sc.wait_ge(sVE.h, rmx_ve)
                    if qp_te_free[qpp]:
                        sc.wait_ge(*qp_te_free[qpp])
                    sc.drain()
                    sc.activation(out=qp_sb[qpp][:, 0:256], in_=bank[bdd][:, 0:256],
                                  func=AF.Exp, bias=biasq[:, 0:1], scale=1.0)
                    sc.drain()
                    sBKC[bdd].inc(sc.activation(
                        out=qp_sb[qpp][:, 256:512], in_=bank[bdd][:, 256:512],
                        func=AF.Exp, bias=biasq[:, 1:2], scale=1.0))
                    sAE.inc(sc.drain())
                on("scalar", b_ex)
                exp_ae = sAE.n
                prev_exp_ae[0] = exp_ae
                if tc == NT - 1:
                    dq_gate_ae[0] = exp_ae

                def b_qtr(t, qpp=qpp, exp_ae=exp_ae):
                    sAE.wait(t, exp_ae)
                    bfree(t, 4)
                    for j in range(4):
                        t.transpose(bkb[4][:, j * 128:(j + 1) * 128],
                                    qp_sb[qpp][:, j * 128:(j + 1) * 128], identb[:])
                    sBKP[4].inc(t.last)
                    qp_te_free[qpp] = (sBKP[4].h, sBKP[4].n)
                on("tensor", b_qtr)

                def b_qte(v, qpp=qpp):
                    sBKP[4].wait(v)
                    v.drain()
                    sBKC[4].inc(v.tensor_copy(qpT[qpp], bkb[4][:, 0:512]))
                    sVE.inc(v.drain())
                on("vector", b_qte)
                qte_ve = sVE.n

                def b_m4(t, pair=pair, tc=tc, qpp=qpp, qte_ve=qte_ve):
                    t.wait_ge(sVE.h, qte_ve)
                    bfree(t, 5)
                    bfree(t, 7)
                    for h2 in range(2):
                        h = pair * 2 + h2
                        bm = bank[5] if h2 == 0 else bank[7]
                        for mc in range(NMC):
                            t.matmul(bm[:, 0:65],
                                     qpT[qpp][:, (h2 * NMC + mc) * 128:(h2 * NMC + mc + 1) * 128],
                                     ctxT[:, (h * NMC + mc) * 65:(h * NMC + mc + 1) * 65],
                                     start=(mc == 0), stop=False)
                        t.matmul(bm[:, 0:65],
                                 onesrowb[0:1, 0:128],
                                 colrow[:, h * 65:(h + 1) * 65],
                                 start=False, stop=True)
                        if h2 == 0:
                            sBKP[5].inc(t.last)
                        else:
                            sBKP[7].inc(t.last)
                on("tensor", b_m4)

                def b_m4r(v):
                    sBKP[5].wait(v)
                    sBKP[7].wait(v)
                    if m4s_prev_ae[0]:
                        v.wait_ge(sAE.h, m4s_prev_ae[0])
                    v.drain()
                    v.reciprocal(dinv2[:, 0:1], bank[5][:, 64:65])
                    sVE.inc(v.reciprocal(dinv2[:, 1:2], bank[7][:, 64:65]))
                on("vector", b_m4r)
                m4r_ve = sVE.n

                def b_m4s(sc, m4r_ve=m4r_ve):
                    sc.wait_ge(sVE.h, m4r_ve)
                    if hp_te_free[0]:
                        sc.wait_ge(*hp_te_free[0])
                    sc.drain()
                    sBKC[5].inc(sc.activation(out=headp[:, 0:64], in_=bank[5][:, 0:64],
                                              func=AF.Copy, bias=0.0, scale=dinv2[:, 0:1]))
                    sc.drain()
                    sBKC[7].inc(sc.activation(out=headp[:, 64:128], in_=bank[7][:, 0:64],
                                              func=AF.Copy, bias=0.0, scale=dinv2[:, 1:2]))
                    sAE.inc(sc.drain())
                on("scalar", b_m4s)
                m4s_ae = sAE.n
                m4s_prev_ae[0] = m4s_ae

                def b_htr(t, m4s_ae=m4s_ae):
                    sAE.wait(t, m4s_ae)
                    bfree(t, 6)
                    t.transpose(bkb[6][:, 0:128], headp[:], identb[:])
                    sBKP[6].inc(t.last)
                    hp_te_free[0] = (sBKP[6].h, sBKP[6].n)
                on("tensor", b_htr)

                def b_hev(v, pair=pair, tc=tc):
                    sBKP[6].wait(v)
                    if pair == 0 and tc == 0:
                        v.wait_ge(sTE.h, A_BAR)
                    v.drain()
                    sBKC[6].inc(v.tensor_copy(
                        attn_feat[:, pair * TC + tc * 128: pair * TC + (tc + 1) * 128],
                        bkb[6][:, 0:128]))
                    sVE.inc(v.drain())
                on("vector", b_hev)

            wq_te_free[par] = (sBKP[1].h, sBKP[1].n)
        B_VE = sVE.n

        def b2_bar(t):
            sTE.inc(t.drain())
        on("tensor", b2_bar)
        B2_BAR = sTE.n

        def b2_wo(g):
            g.wait_ge(sTE.h, B2_BAR)
            sWL.inc(g.dma_start(out=wo_sb.rearrange("p (c j) -> p c j", j=D),
                                in_=wo_d.rearrange("(c p) j -> p c j", p=128)), 16)
        on("gpsimd", b2_wo)
        WO_WL = sWL.n

        def b2_bo(t):
            bfree(t, 7)
            t.matmul(bank[7][:, 0:512], onesrowf[:, :], bo_row[:, 0:512], start=True, stop=True)
            sBKP[7].inc(t.last)
            bfree(t, 5)
            t.matmul(bank[5][:, 0:512], onesrowf[:, :], bo_row[:, 512:1024], start=True, stop=True)
            sBKP[5].inc(t.last)
        on("tensor", b2_bo)

        def b2_bev(v):
            sBKP[7].wait(v)
            sBKP[5].wait(v)
            v.drain()
            sBKC[7].inc(v.tensor_copy(borep[:, 0:512], bank[7][:, 0:512]))
            sBKC[5].inc(v.tensor_copy(borep[:, 512:1024], bank[5][:, 0:512]))
            sVE.inc(v.drain())
        on("vector", b2_bev)
        BOREP_VE = sVE.n

        y1_ve = [0] * NT
        y1_st_n = [0] * NT
        for tc in range(NT):
            par = tc % 2

            def b2_xr(s, tc=tc, par=par):
                if tc >= 2:
                    s.wait_ge(sVE.h, y1_ve[tc - 2])
                elif tc == 0:
                    s.wait_ge(sVE.h, P0_VE)
                sXR[par].inc(s.dma_start(out=xrbuf[par],
                                         in_=x_d[tc * 128:(tc + 1) * 128, :]), 16)
            on("sync", b2_xr)

            def b2_m5(t, tc=tc):
                if tc == 0:
                    t.wait_ge(sWL.h, WO_WL)
                    t.wait_ge(sVE.h, B_VE)
                bfree(t, 0)
                bfree(t, 1)
                for hp in range(NP):
                    t.matmul(bank[0][:, :],
                             attn_feat[:, hp * TC + tc * 128: hp * TC + (tc + 1) * 128],
                             wo_sb[:, hp * D: hp * D + 512],
                             start=(hp == 0), stop=(hp == NP - 1))
                sBKP[0].inc(t.last)
                for hp in range(NP):
                    t.matmul(bank[1][:, :],
                             attn_feat[:, hp * TC + tc * 128: hp * TC + (tc + 1) * 128],
                             wo_sb[:, hp * D + 512: hp * D + 1024],
                             start=(hp == 0), stop=(hp == NP - 1))
                sBKP[1].inc(t.last)
            on("tensor", b2_m5)

            def b2_ye(v, tc=tc, par=par):
                sBKP[0].wait(v)
                sBKP[1].wait(v)
                sXR[par].wait(v)
                if tc <= 1:
                    v.wait_ge(sVE.h, BOREP_VE)
                    v.wait_ge(sTE.h, B2_BAR)
                if tc >= 2:
                    v.wait_ge(sSY1.h, y1_st_n[tc - 1])
                v.drain()
                v.tensor_tensor(y1buf[par][:, 0:512], bank[0][:, :], borep[:, 0:512], ALU.add)
                sBKC[0].inc(v.drain())
                v.tensor_tensor(y1buf[par][:, 512:1024], bank[1][:, :],
                                borep[:, 512:1024], ALU.add)
                sBKC[1].inc(v.drain())
                v.tensor_tensor(y1buf[par], y1buf[par], xrbuf[par], ALU.add)
                sVE.inc(v.drain())
                y1_ve[tc] = sVE.n
            on("vector", b2_ye)

            def b2_st(s, tc=tc, par=par):
                s.wait_ge(sVE.h, y1_ve[tc])
                sSY1.inc(s.dma_start(out=y1_dram[tc * 128:(tc + 1) * 128, :],
                                     in_=y1buf[par]), 16)
                y1_st_n[tc] = sSY1.n
            on("sync", b2_st)

        def c_bar(t):
            sTE.inc(t.drain())
        on("tensor", c_bar)
        C_PE_BAR = sTE.n

        if trunc == 3:
            _stub_final()
            return nc
        if debug >= 3:
            def dbg_y1(g):
                g.wait_ge(sSY1.h, sSY1.n)
                sDBG2.inc(g.dma_start(out=dbg["y1"][:, :], in_=y1_dram[:, :]), 16)
            on("gpsimd", dbg_y1)

        def c_w1(g):
            g.wait_ge(sTE.h, B2_BAR)
            sW1.inc(g.dma_start(out=w1_sb[:, 0:DC * F].rearrange("p (c f) -> p c f", f=F),
                                in_=w1_d.rearrange("(c p) f -> p c f", p=128)), 16)
        on("gpsimd", c_w1)
        W1_WL = sW1.n

        def c_w2(g):
            g.wait_ge(sTE.h, C_PE_BAR)
            sWL.inc(g.dma_start(out=w2_sb.rearrange("p (c j) -> p c j", j=D),
                                in_=w2_d.rearrange("(c p) j -> p c j", p=128)), 16)
        on("gpsimd", c_w2)
        W2_WL = sWL.n

        def c_rld(s):
            sCC3.wait(s)
            s.wait_ge(sSY1.h, sSY1.n)
            SR = TC // TBLF
            for r in range(NCORE):
                sMGL.inc(s.dma_start(
                    out=eidxall[r * SR:(r + 1) * SR, :],
                    in_=msgB_out[r:r + 1, 16:16 + TC].rearrange(
                        "a (s f) -> (a s) f", f=TBLF)), 16)
                sMGL.inc(s.dma_start(
                    out=ptokall[r * SR:(r + 1) * SR, :],
                    in_=msgB_out[r:r + 1, 16 + TC:16 + 2 * TC].rearrange(
                        "a (s f) -> (a s) f", f=TBLF)), 16)
            sMGL.inc(s.dma_start(out=ca8, in_=msgB_out[:, 1:1 + E]), 16)
        on("sync", c_rld)
        RLD_M = sMGL.n

        def c_rk1(v):
            v.wait_ge(sMGL.h, RLD_M)
            v.wait_ge(sTE.h, C_PE_BAR)
            v.drain()
            v.tensor_scalar(maskE, eidxall, cidrep[:, 0:1], None, ALU.is_equal)
            v.drain()
            v.tensor_copy(maskEb, maskE)
            v.tensor_copy(scrA, maskE)
            v.drain()
            d = 1
            src, dst = scrA, scrB
            while d < TBLF:
                v.tensor_copy(dst[:, 0:d], src[:, 0:d])
                v.tensor_tensor(dst[:, d:TBLF], src[:, d:TBLF], src[:, 0:TBLF - d], ALU.add)
                v.drain()
                src, dst = dst, src
                d *= 2
            v.tensor_copy(rowtotb, src[:, TBLF - 1:TBLF])
            v.drain()
            sVE.inc(v.tensor_tensor(rankE, src, maskE, ALU.subtract))
        on("vector", c_rk1)
        RK1_VE = sVE.n

        def c_rk2(t):
            t.wait_ge(sVE.h, RK1_VE)
            bfree(t, 5)
            t.matmul(bank[5][:, 0:1], Lb[:, :], rowtotb, start=True, stop=True)
            sBKP[5].inc(t.last)
        on("tensor", c_rk2)

        def c_rk3(v):
            sBKP[5].wait(v)
            v.drain()
            sBKC[5].inc(v.tensor_scalar(rankE, rankE, bank[5][:, 0:1], None, ALU.add))
            v.drain()
            v.tensor_scalar(scrA, maskE, -BIGOOB, BIGOOB, ALU.mult, ALU.add)
            v.drain()
            v.tensor_tensor(rankE, rankE, scrA, ALU.add)
            v.drain()
            scAi = scrA.bitcast(I32)
            scBi = scrB.bitcast(I32)
            v.tensor_copy(scAi, rankE)
            v.drain()
            v.tensor_scalar(scBi, scAi, 63, None, ALU.bitwise_and)
            v.drain()
            v.tensor_scalar(scAi, scAi, 6, None, ALU.arith_shift_right)
            v.drain()
            v.tensor_copy(loE, scBi)
            v.drain()
            sVE.inc(v.tensor_copy(hiE, scAi))
        on("vector", c_rk3)
        RK3_VE = sVE.n

        for (srcT, dstT) in ((hiE, hiT), (loE, loT), (ptokall, gTT)):
            def c_tt(t, srcT=srcT):
                sVE.wait(t)
                bfree(t, 3)
                for j in range(NTT):
                    t.transpose(bank[3][:, j * 128:(j + 1) * 128],
                                srcT[:, j * 128:(j + 1) * 128], identf[:])
                sBKP[3].inc(t.last)
            on("tensor", c_tt)

            def c_tv(v, dstT=dstT):
                sBKP[3].wait(v)
                v.drain()
                for j in range(NTT):
                    v.tensor_copy(dstT[:, j * 128:(j + 1) * 128],
                                  bank[3][:, j * 128:(j + 1) * 128])
                sBKC[3].inc(v.drain())
                sVE.inc(v.drain())
            on("vector", c_tv)
        TBLT_VE = sVE.n

        oh_te = [0]
        for c in range(NCH):
            pidx = (c * 128) // TBLF
            tj = ((c * 128) % TBLF) // 128
            ccol = tj * 128 + pidx

            def c_oh(v, c=c, ccol=ccol):
                if c == 0:
                    v.wait_ge(sVE.h, TBLT_VE)
                else:
                    v.wait_ge(sTE.h, oh_te[0])
                v.drain()
                v.tensor_scalar(ohhi[:], hiio[:], hiT[:, ccol:ccol + 1], None, ALU.is_equal)
                v.tensor_scalar(ohblk[:, 0:64], hiio[:], loT[:, ccol:ccol + 1],
                                None, ALU.is_equal)
                v.drain()
                v.tensor_scalar(ohblk[:, 64:128], ohblk[:, 0:64], float(c >> 1),
                                None, ALU.mult)
                v.tensor_scalar(ohblk[:, 128:192], ohblk[:, 0:64],
                                plo1[:] if (c & 1) else piota[:], None, ALU.mult)
                v.tensor_scalar(ohblk[:, 192:256], ohblk[:, 0:64],
                                gTT[:, ccol:ccol + 1], None, ALU.mult)
                sVE.inc(v.drain())
            on("vector", c_oh)
            oh_ve = sVE.n

            def c_ohm(t, c=c, oh_ve=oh_ve):
                t.wait_ge(sVE.h, oh_ve)
                if c == 0:
                    bfree(t, 5)
                t.matmul(bank[5][0:64, 0:256], ohhi[:, :], ohblk[:, :],
                         start=(c == 0), stop=(c == NCH - 1))
                if c == NCH - 1:
                    sBKP[5].inc(t.last)
                else:
                    sTE.inc(t.last)
                    oh_te[0] = sTE.n
            on("tensor", c_ohm)

        def c_dt1(v):
            sBKP[5].wait(v)
            v.drain()
            sBKC[5].inc(v.tensor_copy(dtab[:, :], bank[5][0:64, 0:256]))
            v.drain()
            v.tensor_scalar(dfull[:, :], dtab[:, 64:128], 256.0, None, ALU.mult)
            v.drain()
            v.tensor_tensor(dfull[:, :], dfull[:, :], dtab[:, 128:192], ALU.add)
            v.drain()
            v.tensor_scalar(scrA[0:64, 0:64], dtab[:, 0:64], -BIGOOB, BIGOOB,
                            ALU.mult, ALU.add)
            v.drain()
            sVE.inc(v.tensor_tensor(dfull[:, :], dfull[:, :], scrA[0:64, 0:64], ALU.add))
        on("vector", c_dt1)
        DT1_VE = sVE.n

        if debug >= 4:
            def dbg_dt(s):
                s.wait_ge(sVE.h, DT1_VE)
                sDBG.inc(s.dma_start(out=dbg["dtab"][:, :], in_=dtab[:, :]), 16)
            on("sync", dbg_dt)

        def c_dt2(t):
            t.wait_ge(sVE.h, DT1_VE)
            bfree(t, 7)
            t.transpose(bank[7][0:64, 0:64], dfull[:, :], identf[0:64, 0:64])
            t.transpose(bank[7][0:64, 64:128], dtab[:, 192:256], identf[0:64, 0:64])
            sBKP[7].inc(t.last)
        on("tensor", c_dt2)

        def c_dt3(v):
            sBKP[7].wait(v)
            v.drain()
            v.tensor_copy(dfT[:, :], bank[7][0:64, 0:64])
            sBKC[7].inc(v.tensor_copy(gcq[:, :], bank[7][0:64, 64:128]))
            v.drain()
            v.tensor_copy(dfTi[:, :], dfT[:, :])
            sVE.inc(v.tensor_copy(gcol[0:64, 0:CAP // 128],
                                  gcq[:, 0:CAP // 64][:, 0::2]))
        on("vector", c_dt3)
        DT3_VE = sVE.n

        def c_gc(g):
            g.wait_ge(sVE.h, DT3_VE)
            sGC.inc(g.dma_start(out=gcol[64:128, 0:CAP // 128],
                                in_=gcq[:, 0:CAP // 64][:, 1::2],
                                allow_slow_non_contiguous=True), 16)
        on("gpsimd", c_gc)

        xe_te_free = [None, None]
        hb_te_mark = [0]
        ye_st_mark = [0, 0]

        for gi in range(NG):
            xet_marks = []
            for j2 in range(2):
                jj = gi * 2 + j2

                def c_gat(g, gi=gi, j2=j2, jj=jj):
                    if gi == 0 and j2 == 0:
                        sCC1.wait(g)
                        g.wait_ge(sVE.h, DT3_VE)
                        g.wait_ge(sW1.h, W1_WL)
                    if xe_te_free[j2]:
                        g.wait_ge(*xe_te_free[j2])
                    sGE.inc(g.memset(xe_tok[j2], 0.0))
                    sGE.wait(g)
                    for half in range(2):
                        sGA[j2].inc(g.indirect_dma_start(
                            out=xe_tok[j2][half * 64:(half + 1) * 64, :],
                            out_offset=None,
                            in_=xnall[:, :],
                            in_offset=bass.IndirectOffsetOnAxis(
                                ap=dfTi[:, 2 * jj + half: 2 * jj + half + 1], axis=0),
                            bounds_check=bc_reg[0], oob_is_err=False), 16)
                on("gpsimd", c_gat)
                ga_n = sGA[j2].n

                def c_xtr(t, j2=j2, ga_n=ga_n):
                    t.wait_ge(sGA[j2].h, ga_n)
                    bfree(t, 4)
                    for c in range(DC):
                        t.transpose(bkb[4][:, c * 128:(c + 1) * 128],
                                    xe_tok[j2][:, c * 128:(c + 1) * 128], identb[:])
                    sBKP[4].inc(t.last)
                    xe_te_free[j2] = (sBKP[4].h, sBKP[4].n)
                on("tensor", c_xtr)

                def c_xte(v, j2=j2, gi=gi):
                    sBKP[4].wait(v)
                    if gi == 0:
                        v.wait_ge(sTE.h, C_PE_BAR)
                    v.drain()
                    for c in range(DC):
                        v.tensor_copy(xeT[:, c * 256 + j2 * 128: c * 256 + j2 * 128 + 128],
                                      bkb[4][:, c * 128:(c + 1) * 128])
                    sBKC[4].inc(v.drain())
                    sVE.inc(v.drain())
                on("vector", c_xte)
            xet_ve = sVE.n

            for fc in range(FC):
                bm = fc % 2

                def c_m6(t, gi=gi, fc=fc, bm=bm, xet_ve=xet_ve):
                    if fc == 0:
                        t.wait_ge(sVE.h, xet_ve)
                        if gi == 0:
                            t.wait_ge(sW1.h, W1_WL)
                    bfree(t, bm)
                    for c in range(DC):
                        t.matmul(bank[bm][:, 0:256],
                                 w1_sb[:, c * F + fc * 128: c * F + (fc + 1) * 128],
                                 xeT[:, c * 256:(c + 1) * 256],
                                 start=(c == 0), stop=(c == DC - 1))
                    sBKP[bm].inc(t.last)
                on("tensor", c_m6)

                if fc % 2 == 0:
                    def c_rl(v, gi=gi, fc=fc, bm=bm):
                        sBKP[bm].wait(v)
                        if gi == 0 and fc < 2:
                            v.wait_ge(sTE.h, C_PE_BAR)
                        if hb_te_mark[0] and fc < 2:
                            v.wait_ge(sTE.h, hb_te_mark[0])
                        v.drain()
                        sBKC[bm].inc(v.tensor_scalar(hbuf[:, fc * 256:(fc + 1) * 256],
                                                     bank[bm][:, 0:256], 0.0, None, ALU.max))
                        sVE.inc(v.drain())
                    on("vector", c_rl)
                else:
                    def c_rl(sc, gi=gi, fc=fc, bm=bm):
                        sBKP[bm].wait(sc)
                        if gi == 0 and fc < 2:
                            sc.wait_ge(sTE.h, C_PE_BAR)
                        if hb_te_mark[0] and fc < 2:
                            sc.wait_ge(sTE.h, hb_te_mark[0])
                        sc.drain()
                        sBKC[bm].inc(sc.activation(out=hbuf[:, fc * 256:(fc + 1) * 256],
                                                   in_=bank[bm][:, 0:256], func=AF.Relu,
                                                   bias=0.0, scale=1.0))
                        sAE.inc(sc.drain())
                    on("scalar", c_rl)
            h_ve = sVE.n
            h_ae = sAE.n

            for sc2 in range(2):
                for dot in range(2):
                    bm7 = 2 + dot

                    def c_m7(t, gi=gi, sc2=sc2, dot=dot, bm7=bm7, h_ve=h_ve, h_ae=h_ae):
                        if gi == 0 and sc2 == 0 and dot == 0:
                            t.wait_ge(sWL.h, W2_WL)
                        t.wait_ge(sVE.h, h_ve)
                        t.wait_ge(sAE.h, h_ae)
                        bfree(t, bm7)
                        for fc in range(FC):
                            t.matmul(bank[bm7][:, :],
                                     hbuf[:, fc * 256 + sc2 * 128: fc * 256 + sc2 * 128 + 128],
                                     w2_sb[:, fc * D + dot * 512: fc * D + (dot + 1) * 512],
                                     start=(fc == 0), stop=(fc == FC - 1))
                        sBKP[bm7].inc(t.last)
                        if gi == 0 and sc2 == 0 and dot == 0:
                            pass
                    on("tensor", c_m7)
                    bkp_n = sBKP[bm7].n

                    def c_yev(sc, gi=gi, sc2=sc2, dot=dot, bm7=bm7, bkp_n=bkp_n):
                        sc.wait_ge(sBKP[bm7].h, bkp_n)
                        if gi == 0 and sc2 == 0 and dot == 0:
                            sGC.wait(sc)
                            sc.wait_ge(sWL.h, W2_WL)
                            sc.wait_ge(sTE.h, C_PE_BAR)
                        if ye_st_mark[sc2] and dot == 0:
                            sc.wait_ge(sSYE.h, sSYE.n)
                        sc.drain()
                        sBKC[bm7].inc(sc.activation(
                            out=yebuf[sc2][:, dot * 512:(dot + 1) * 512],
                            in_=bank[bm7][:, :], func=AF.Copy, bias=0.0,
                            scale=gcol[:, gi * 2 + sc2: gi * 2 + sc2 + 1]))
                        sAE.inc(sc.drain())
                    on("scalar", c_yev)
                ye_ae = sAE.n

                def c_yst(s, gi=gi, sc2=sc2, ye_ae=ye_ae):
                    s.wait_ge(sAE.h, ye_ae)
                    sSYE.inc(s.dma_start(
                        out=ye_dram[gi * 256 + sc2 * 128: gi * 256 + (sc2 + 1) * 128, :],
                        in_=yebuf[sc2]), 16)
                on("sync", c_yst)
                ye_st_mark[sc2] = sSYE.n

            def c_hbm(t):
                sTE.inc(t.nop())
                hb_te_mark[0] = sTE.n
            on("tensor", c_hbm)

        def c_agy(g):
            g.wait_ge(sSYE.h, sSYE.n)
            sCC4.inc(g.collective_compute(
                "AllGather", ALU.bypass, replica_groups=[list(range(NCORE))],
                ins=[ye_dram.opt()], outs=[yeall.opt()]))
        on("gpsimd", c_agy)
        if trunc == 4:
            _stub_final()
            return nc

        def f_cnt(t):
            bfree(t, 5)
            t.matmul(bank[5][0:1, 0:E * NT], ones128b[:, :], maskbig[:],
                     start=True, stop=True)
            sBKP[5].inc(t.last)
        on("tensor", f_cnt)

        def f_cntv(v):
            sBKP[5].wait(v)
            v.drain()
            sBKC[5].inc(v.tensor_copy(cnte[:], bank[5][0:1, 0:E * NT]))
            sVE.inc(v.drain())
        on("vector", f_cntv)

        def f_cpre(v):
            v.drain()
            v.memset(cntp[:], 0.0)
            v.drain()
            ve = cntp.rearrange("a (e t) -> a e t", t=NT)
            se = cnte.rearrange("a (e t) -> a e t", t=NT)
            v.tensor_copy(ve[:, :, 1:NT], se[:, :, 0:NT - 1])
            v.drain()
            d = 1
            src, dst = cntp, cnte
            while d < NT:
                sv = src.rearrange("a (e t) -> a e t", t=NT)
                dv = dst.rearrange("a (e t) -> a e t", t=NT)
                v.tensor_copy(dv[:, :, 0:d], sv[:, :, 0:d])
                v.tensor_tensor(dv[:, :, d:NT], sv[:, :, d:NT], sv[:, :, 0:NT - d], ALU.add)
                v.drain()
                src, dst = dst, src
                d *= 2
            if src is not cntp:
                v.tensor_copy(cntp[:], cnte[:])
                v.drain()
            v.tensor_scalar(ltm8[:, :], piota[0:NCORE, 0:1], cidrep[0:NCORE, 0:1],
                            None, ALU.is_lt)
            v.drain()
            v.wait_ge(sMGL.h, RLD_M)
            sVE.inc(v.tensor_scalar(ca8[:, :], ca8[:, :], ltm8[:, 0:1], None, ALU.mult))
        on("vector", f_cpre)
        FC_VE = sVE.n

        def f_offs(t):
            t.wait_ge(sVE.h, FC_VE)
            bfree(t, 7)
            t.matmul(bank[7][0:1, 32:32 + E], onescolf[0:NCORE, 0:1], ca8[:, :],
                     start=True, stop=True)
            sBKP[7].inc(t.last)
        on("tensor", f_offs)

        def f_ov(v):
            sBKP[7].wait(v)
            v.drain()
            sBKC[7].inc(v.tensor_copy(offs8[:, :], bank[7][0:1, 32:32 + E]))
            v.drain()
            sVE.inc(v.memset(rmine, 0.0))
        on("vector", f_ov)
        FO_VE = sVE.n

        for e in range(E):
            def f_re(v, e=e):
                if e == 0:
                    v.wait_ge(sVE.h, FO_VE)
                v.drain()
                sVE.inc(v.tensor_scalar(rowe[:, :], cntp[:, e * NT:(e + 1) * NT],
                                        offs8[:, e:e + 1], None, ALU.add))
            on("vector", f_re)
            re_ve = sVE.n

            def f_rm(t, e=e, re_ve=re_ve):
                t.wait_ge(sVE.h, re_ve)
                bfree(t, 5)
                t.matmul(bank[5][:, 0:NT], Lb[:, :], maskbig[:, e * NT:(e + 1) * NT],
                         start=True, stop=True)
                sBKP[5].inc(t.last)
                bfree(t, 7)
                t.matmul(bank[7][:, 32:32 + NT], onesrowf[:, :], rowe[:, :],
                         start=True, stop=True)
                sBKP[7].inc(t.last)
            on("tensor", f_rm)

            def f_rv(v, e=e):
                sBKP[5].wait(v)
                sBKP[7].wait(v)
                v.drain()
                sBKC[5].inc(v.tensor_copy(le_sb[:], bank[5][:, 0:NT]))
                v.drain()
                sBKC[7].inc(v.tensor_tensor(le_sb[:], le_sb[:], bank[7][:, 32:32 + NT],
                                            ALU.add))
                v.drain()
                v.tensor_tensor(rk_u, maskf[:, e * NT:(e + 1) * NT], le_sb[:], ALU.mult)
                v.drain()
                v.tensor_tensor(rk_t, rmine, rk_u, ALU.add)
                v.drain()
                sVE.inc(v.tensor_copy(rmine, rk_t))
            on("vector", f_rv)
        RM_VE = sVE.n

        if debug >= 4:
            def dbg_rm(s):
                s.wait_ge(sVE.h, RM_VE)
                sDBG.inc(s.dma_start(out=dbg["rankm"][:, :], in_=rmine), 16)
            on("sync", dbg_rm)

        def f_idx(v):
            v.drain()
            v.tensor_scalar(rk_u, rmine, float(CAP), None, ALU.is_ge)
            v.drain()
            v.tensor_scalar(rk_t, rk_u, BIGOOB, None, ALU.mult)
            v.drain()
            v.tensor_scalar(rk_s, eidx_mine[:, :], float(CAP), None, ALU.mult)
            v.drain()
            v.tensor_tensor(rk_s, rk_s, rmine, ALU.add)
            v.drain()
            v.tensor_tensor(rk_s, rk_s, rk_t, ALU.add)
            v.drain()
            sVE.inc(v.tensor_copy(idx2i[:, :], rk_s))
        on("vector", f_idx)
        IDX_VE = sVE.n

        fin_ve = [0] * NT
        out_st_n = [0] * NT
        for tc in range(NT):
            par = tc % 2

            def f_y1l(s, tc=tc, par=par):
                if tc >= 2:
                    s.wait_ge(sVE.h, fin_ve[tc - 2])
                elif tc == 0:
                    s.wait_ge(sVE.h, RM_VE)
                sY1L[par].inc(s.dma_start(out=xrbuf[par],
                                          in_=y1_dram[tc * 128:(tc + 1) * 128, :]), 16)
            on("sync", f_y1l)

            def f_gye(g, tc=tc, par=par):
                if tc == 0:
                    sCC4.wait(g)
                    g.wait_ge(sVE.h, IDX_VE)
                    g.wait_ge(sTE.h, C_PE_BAR)
                if tc >= 2:
                    g.wait_ge(sVE.h, fin_ve[tc - 2])
                sGE.inc(g.memset(yegb[par], 0.0))
                sGE.wait(g)
                sGB[par].inc(g.indirect_dma_start(
                    out=yegb[par], out_offset=None, in_=yeall[:, :],
                    in_offset=bass.IndirectOffsetOnAxis(ap=idx2i[:, tc:tc + 1], axis=0),
                    bounds_check=bc_reg[0], oob_is_err=False), 16)
            on("gpsimd", f_gye)
            gb_n = sGB[par].n

            def f_add(v, tc=tc, par=par, gb_n=gb_n):
                sY1L[par].wait(v)
                v.wait_ge(sGB[par].h, gb_n)
                if tc >= 2:
                    v.wait_ge(sOUT.h, out_st_n[tc - 1])
                v.drain()
                sVE.inc(v.tensor_tensor(y1buf[par], xrbuf[par], yegb[par], ALU.add))
                fin_ve[tc] = sVE.n
            on("vector", f_add)

            def f_st(s, tc=tc, par=par):
                s.wait_ge(sVE.h, fin_ve[tc])
                sOUT.inc(s.dma_start(out=y_d[tc * 128:(tc + 1) * 128, :],
                                     in_=y1buf[par]), 16)
                out_st_n[tc] = sOUT.n
            on("sync", f_st)

        def f_wait(s):
            s.wait_ge(sOUT.h, sOUT.n)
            if dbg:
                s.wait_ge(sDBG.h, sDBG.n)
                s.wait_ge(sDBG2.h, sDBG2.n)
        on("sync", f_wait)

    return nc


def _stage_inputs(inputs, N=None):
    N = N_ if N is None else N
    T = B_ * N
    TCl = T // NCORE
    x = np.ascontiguousarray(np.asarray(inputs["L0_attn_x"], dtype=np.float32)
                             .reshape(T, D_))
    base = {
        "wq": np.ascontiguousarray(np.asarray(inputs["L0_attn_Wq"], dtype=np.float32)),
        "wk": np.ascontiguousarray(np.asarray(inputs["L0_attn_Wk"], dtype=np.float32)),
        "wv": np.ascontiguousarray(np.asarray(inputs["L0_attn_Wv"], dtype=np.float32)),
        "wo": np.ascontiguousarray(np.asarray(inputs["L0_attn_Wo"], dtype=np.float32)),
        "bo": np.ascontiguousarray(np.asarray(inputs["L0_attn_bo"], dtype=np.float32)
                                   .reshape(1, D_)),
        "pmat": np.ascontiguousarray(np.asarray(inputs["L0_attn_P"], dtype=np.float32)),
        "wr": np.ascontiguousarray(np.asarray(inputs["L0_ffn_Wr"], dtype=np.float32)),
    }
    w1 = np.asarray(inputs["L0_ffn_W1"], dtype=np.float32)
    w2 = np.asarray(inputs["L0_ffn_W2"], dtype=np.float32)
    in_maps = []
    for c in range(NCORE):
        m = dict(base)
        m["x"] = np.ascontiguousarray(x[c * TCl:(c + 1) * TCl])
        m["w1"] = np.ascontiguousarray(w1[c])
        m["w2"] = np.ascontiguousarray(w2[c])
        m["cid"] = np.array([[float(c)]], dtype=np.float32)
        in_maps.append(m)
    return in_maps


def kernel(**inputs):
    key = (DEBUG,)
    if key not in _CACHE:
        _CACHE[key] = build(debug=DEBUG)
    nc = _CACHE[key]
    in_maps = _stage_inputs(inputs)
    res = run_bass_kernel_spmd(nc, in_maps, core_ids=list(range(NCORE)))
    out = np.concatenate([np.asarray(r["y"]) for r in res.results], axis=0)
    return out.reshape(B_, N_, D_)


# version 06 (final)
# speedup vs baseline: 1.1146x; 1.1146x over previous
import math
import numpy as np
from contextlib import ExitStack

import concourse.bass as bass
import concourse.mybir as mybir
from concourse.bass_utils import run_bass_kernel_spmd

F32 = mybir.dt.float32
BF16 = mybir.dt.bfloat16
I32 = mybir.dt.int32
AF = mybir.ActivationFunctionType
ALU = mybir.AluOpType
AX = mybir.AxisListType

NCORE = 8
B_, N_, D_ = 4, 8192, 1024
H_, DH_, M_ = 16, 64, 256
E_, F_ = 8, 4096
DEBUG = 0
TRUNC = 0

LN_EPS = 1e-5
EPS_PERF = 1e-4
BIGOOB = 1.0e6

_CACHE = {}


def build(debug=0, N=None, trunc=0):
    N = N_ if N is None else N
    D, H, DH, M, E, F = D_, H_, DH_, M_, E_, F_
    T = B_ * N
    TC = T // NCORE
    CAP = T // E
    NT = TC // 128
    NTB = TC // 512
    DC = D // 128
    NP = H // 2
    NCH = T // 128
    TBLF = T // 128
    NTT = TBLF // 128
    NG = CAP // 256
    FC = F // 128
    NMC = M // 128
    DN = float(DH) ** -0.25
    RATIO = float(M) ** -0.5
    LNR = math.log(RATIO)
    assert NTT >= 1

    nc = bass.Bass("TRN2", target_bir_lowering=False, debug=False)

    x_d = nc.dram_tensor("x", [TC, D], F32, kind="ExternalInput").ap()
    wq_d = nc.dram_tensor("wq", [D, D], F32, kind="ExternalInput").ap()
    wk_d = nc.dram_tensor("wk", [D, D], F32, kind="ExternalInput").ap()
    wv_d = nc.dram_tensor("wv", [D, D], F32, kind="ExternalInput").ap()
    wo_d = nc.dram_tensor("wo", [D, D], F32, kind="ExternalInput").ap()
    bo_d = nc.dram_tensor("bo", [1, D], F32, kind="ExternalInput").ap()
    p_d = nc.dram_tensor("pmat", [M, DH], F32, kind="ExternalInput").ap()
    wr_d = nc.dram_tensor("wr", [D, E], F32, kind="ExternalInput").ap()
    w1_d = nc.dram_tensor("w1", [D, F], F32, kind="ExternalInput").ap()
    w2_d = nc.dram_tensor("w2", [F, D], F32, kind="ExternalInput").ap()
    cid_d = nc.dram_tensor("cid", [1, 1], F32, kind="ExternalInput").ap()
    y_d = nc.dram_tensor("y", [TC, D], F32, kind="ExternalOutput").ap()

    xn_dram = nc.dram_tensor("xn_dram", [TC, D], BF16, kind="Internal").ap()
    xnall = nc.dram_tensor("xnall", [T, D], BF16, kind="Internal").ap()
    msgA_in = nc.dram_tensor("msgA_in", [DH + 1, H * M + H], F32, kind="Internal").ap()
    msgA_out = nc.dram_tensor("msgA_out", [DH + 1, H * M + H], F32, kind="Internal").ap()
    MBW = 16 + TC + TC
    msgB_in = nc.dram_tensor("msgB_in", [1, MBW], F32, kind="Internal").ap()
    msgB_out = nc.dram_tensor("msgB_out", [NCORE, MBW], F32, kind="Internal").ap()
    ye_dram = nc.dram_tensor("ye_dram", [CAP, D], BF16, kind="Internal").ap()
    yeall = nc.dram_tensor("yeall", [T, D], BF16, kind="Internal").ap()
    y1_dram = nc.dram_tensor("y1_dram", [TC, D], F32, kind="Internal").ap()

    dbg = {}
    if debug >= 1:
        dbg["eidx"] = nc.dram_tensor("dbg_eidx", [128, NT], F32, kind="ExternalOutput").ap()
        dbg["ptok"] = nc.dram_tensor("dbg_ptok", [128, NT], F32, kind="ExternalOutput").ap()
        dbg["xn"] = nc.dram_tensor("dbg_xn", [TC, D], F32, kind="ExternalOutput").ap()
    if debug >= 2:
        dbg["ctx"] = nc.dram_tensor("dbg_ctx", [DH + 1, H * M + H], F32, kind="ExternalOutput").ap()
        dbg["mx"] = nc.dram_tensor("dbg_mx", [1, 1], F32, kind="ExternalOutput").ap()
    if debug >= 3:
        dbg["y1"] = nc.dram_tensor("dbg_y1", [TC, D], F32, kind="ExternalOutput").ap()
    if debug >= 4:
        dbg["dtab"] = nc.dram_tensor("dbg_dtab", [64, 256], F32, kind="ExternalOutput").ap()
        dbg["rankm"] = nc.dram_tensor("dbg_rankm", [128, NT], F32, kind="ExternalOutput").ap()

    stack = ExitStack()
    with stack:
        def sbuf(name, shape, dt):
            return stack.enter_context(nc.sbuf_tensor(name, shape, dt))

        class Sem:
            __slots__ = ("h", "n")

            def __init__(self, name):
                self.h = stack.enter_context(nc.semaphore(name))
                self.n = 0

            def inc(self, inst, k=1):
                inst.then_inc(self.h, k)
                self.n += k
                return self.n

            def wait(self, eng, n=None):
                eng.wait_ge(self.h, self.n if n is None else n)

        block = stack.enter_context(nc.Block())

        class EngP:
            __slots__ = ("_e", "last")

            def __init__(self, e):
                self._e = e
                self.last = None

            def __getattr__(self, name):
                f = getattr(self._e, name)
                if not callable(f) or name in ("nop", "wait_ge"):
                    return f

                def w(*a, **k):
                    r = f(*a, **k)
                    self.last = r
                    return r
                return w

        def on(engname, fn):
            getattr(block, engname)(lambda e: fn(EngP(e)))

        bank = {}
        for i in (0, 1, 2, 3, 5, 7):
            bank[i] = stack.enter_context(nc.psum_tensor(f"pb{i}", [128, 512], F32)).ap()
        bkb = {}
        for i in (4, 6):
            bkb[i] = stack.enter_context(nc.psum_tensor(f"pb{i}", [128, 1024], BF16)).ap()
        bkf = {i: bkb[i].bitcast(F32) for i in (4, 6)}

        sBKP = [Sem(f"sBKP{i}") for i in range(8)]
        sBKC = [Sem(f"sBKC{i}") for i in range(8)]

        def bfree(eng, i):
            eng.wait_ge(sBKC[i].h, sBKC[i].n)

        sTE = Sem("sTE")
        sVE = Sem("sVE")
        sAE = Sem("sAE")
        sGE = Sem("sGE")
        sWL = Sem("sWL")
        sXN = Sem("sXN")
        sST = Sem("sST")
        sSY1 = Sem("sSY1")
        sSYE = Sem("sSYE")
        sMSG = Sem("sMSG")
        sCC1 = Sem("sCC1")
        sCC2 = Sem("sCC2")
        sCC3 = Sem("sCC3")
        sCC4 = Sem("sCC4")
        sMGL = Sem("sMGL")
        sGA = [Sem("sGA0"), Sem("sGA1")]
        sGB = [Sem("sGB0"), Sem("sGB1")]
        sGC = Sem("sGC")
        sOUT = Sem("sOUT")
        sDBG = Sem("sDBG")
        sDBG2 = Sem("sDBG2")
        sXL = [Sem("sXL0"), Sem("sXL1")]
        sWQ = [Sem("sWQ0"), Sem("sWQ1")]
        sW1 = Sem("sW1")
        sXR = [Sem("sXR0"), Sem("sXR1")]
        sY1L = [Sem("sY1L0"), Sem("sY1L1")]

        AB1 = sbuf("AB1", [128, max(DC * TC, DC * F)], BF16)
        AB2C = max(NP * TC, 32 * D, 2 * (4 * H * 65) + DC * D)
        AB2 = sbuf("AB2", [128, AB2C], BF16)
        AB3 = sbuf("AB3", [128, 16 * 1024], BF16)
        AF1 = sbuf("AF1", [128, 3 * D], F32)
        AF1b = sbuf("AF1b", [128, D], F32)
        AF1c = sbuf("AF1c", [128, D], F32)
        AFC = sbuf("AFC", [128, max(H * M, 11 * TBLF + 4 * (TC // 128) + 1024)], F32)

        xnT = AB1.ap()
        w1_sb = AB1.ap()
        attn_feat = AB2.ap()
        w2_sb = AB2.ap()[:, 0:32 * D]
        vp = [AB2.ap()[:, i * (4 * H * 65):(i + 1) * (4 * H * 65)] for i in range(2)]
        wkfull = AB2.ap()[:, 2 * (4 * H * 65):2 * (4 * H * 65) + DC * D]
        _ab2f = 2 * (4 * H * 65) + DC * D
        kblk = [AB2.ap()[:, _ab2f + i * 512: _ab2f + (i + 1) * 512] for i in range(2)]
        kp_sb = [AB2.ap()[:, _ab2f + 1024 + i * 512: _ab2f + 1024 + (i + 1) * 512] for i in range(2)]
        k2b = AB2.ap()[:, _ab2f + 2048: _ab2f + 2560]
        wv_sb = AB3.ap()[:, 0:DC * D]
        wo_sb = AB3.ap()[:, 0:NP * D]
        qblk = AB3.ap()[:, 0:TC]
        qp_sb = [AB3.ap()[:, TC + i * 512: TC + (i + 1) * 512] for i in range(2)]
        qpT = [AB3.ap()[:, TC + 1024 + i * 512: TC + 1024 + (i + 1) * 512] for i in range(2)]
        headp = AB3.ap()[:, TC + 2048: TC + 2176]
        wq_sb = [AB3.ap()[:, TC + 2304 + i * D: TC + 2304 + (i + 1) * D] for i in range(2)]
        ctxT = AB3.ap()[:, TC + 2304 + 2 * D: TC + 2304 + 2 * D + H * NMC * 65]
        colrow = AB3.ap()[0:1, TC + 2304 + 2 * D + H * NMC * 65:
                          TC + 2304 + 2 * D + H * NMC * 65 + H * 65]
        bo_row = AF1b.ap()[0:1, 0:D]
        diagp = AF1c.ap()[0:2, 0:512]
        xloT = [AB3.ap()[:, 12288 + i * D: 12288 + (i + 1) * D] for i in range(2)]
        xnbf = [AB3.ap()[:, 14336 + i * D: 14336 + (i + 1) * D] for i in range(2)]
        hbuf = AB3.ap()[:, 0:32 * 256]
        xeT = AB3.ap()[:, 8192:8192 + DC * 256]
        xe_tok = [AB3.ap()[:, 10240 + i * D: 10240 + (i + 1) * D] for i in range(2)]
        yebuf = [AB3.ap()[:, 12288 + i * D: 12288 + (i + 1) * D] for i in range(2)]
        yegb = [AB3.ap()[:, 14336 + i * D: 14336 + (i + 1) * D] for i in range(2)]
        xbuf = [AF1.ap()[:, 0:D], AF1.ap()[:, D:2 * D]]
        xnfp = AF1.ap()[:, 2 * D:3 * D]
        xrbuf = [AF1.ap()[:, 0:D], AF1.ap()[:, D:2 * D]]
        y1buf = [AF1.ap()[:, 2 * D:3 * D], AF1c.ap()[:, 0:D]]
        borep = AF1b.ap()[:, 0:D]
        ctx_acc = AFC.ap()[0:DH + 1, :]
        ctxF = ctx_acc
        rankE = AFC.ap()[:, 0:TBLF]
        hiE = AFC.ap()[:, TBLF:2 * TBLF]
        loE = AFC.ap()[:, 2 * TBLF:3 * TBLF]
        hiT = AFC.ap()[:, 3 * TBLF:4 * TBLF]
        loT = AFC.ap()[:, 4 * TBLF:5 * TBLF]
        gTT = AFC.ap()[:, 5 * TBLF:6 * TBLF]
        scrA = AFC.ap()[:, 6 * TBLF:7 * TBLF]
        scrB = AFC.ap()[:, 7 * TBLF:8 * TBLF]
        eidxall = AFC.ap()[:, 8 * TBLF:9 * TBLF]
        ptokall = AFC.ap()[:, 9 * TBLF:10 * TBLF]
        maskE = AFC.ap()[:, 10 * TBLF:11 * TBLF]
        rmine = AFC.ap()[:, 11 * TBLF:11 * TBLF + NT]
        rk_u = AFC.ap()[:, 11 * TBLF + NT:11 * TBLF + 2 * NT]
        rk_t = AFC.ap()[:, 11 * TBLF + 2 * NT:11 * TBLF + 3 * NT]
        rk_s = AFC.ap()[:, 11 * TBLF + 3 * NT:11 * TBLF + 4 * NT]
        _cf = 11 * TBLF + 4 * NT
        def _take(n):
            nonlocal _cf
            a = _cf
            _cf += n
            return a
        _a = _take(256); dtab = AFC.ap()[0:64, _a:_a + 256]
        _a = _take(64); dfull = AFC.ap()[0:64, _a:_a + 64]
        _a = _take(64); dfT = AFC.ap()[0:64, _a:_a + 64]
        _a = _take(64); gcq = AFC.ap()[0:64, _a:_a + 64]
        _a = _take(TBLF // 2); maskEb = AFC.ap()[:, _a:_a + TBLF // 2].bitcast(BF16)
        _a = _take(32); ohhi = AFC.ap()[:, _a:_a + 32].bitcast(BF16)
        _a = _take(128); ohblk = AFC.ap()[:, _a:_a + 128].bitcast(BF16)
        _a = _take(1); rowtotb = AFC.ap()[:, _a:_a + 1].bitcast(BF16)[:, 0:1]
        cnte = AFC.ap()[0:1, 6 * TBLF:6 * TBLF + E * NT]
        cntp = AFC.ap()[0:1, 7 * TBLF:7 * TBLF + E * NT]
        _a = _take(max(CAP // 128, 2)); gcol = AFC.ap()[:, _a:_a + max(CAP // 128, 2)]
        _a = _take(E); ca8 = AFC.ap()[0:NCORE, _a:_a + E]
        _a = _take(1); ltm8 = AFC.ap()[0:NCORE, _a:_a + 1]
        _a = _take(E); offs8 = AFC.ap()[0:1, _a:_a + E]
        _a = _take(NT); rowe = AFC.ap()[0:1, _a:_a + NT]
        _a = _take(NT); le_sb = AFC.ap()[:, _a:_a + NT]
        assert _cf <= max(H * M, 11 * TBLF + 4 * NT + 1024), _cf

        identb = sbuf("identb", [128, 128], BF16).ap()
        identf = sbuf("identf", [128, 128], F32).ap()
        ones128b = sbuf("ones128b", [128, 1], BF16).ap()
        onescolf = sbuf("onescolf", [128, 1], F32).ap()
        onesrowb = sbuf("onesrowb", [1, 512], BF16).ap()
        onesrowf = sbuf("onesrowf", [1, 128], F32).ap()
        ones2b = sbuf("ones2b", [128, 2], BF16).ap()
        Lb = sbuf("Lb", [128, 128], BF16).ap()
        ptbd = sbuf("ptbd", [128, 2 * M], BF16).ap()
        cidrep = sbuf("cidrep", [128, 1], F32).ap()
        e1000 = sbuf("e1000", [128, E], F32).ap()
        hiio = sbuf("hiio", [128, 64], F32).ap()
        piota = sbuf("piota", [128, 1], F32).ap()
        plo1 = sbuf("plo1", [128, 1], F32).ap()
        wrhl = sbuf("wrhl", [128, 2 * E * DC], BF16).ap()
        wrhi = sbuf("wrhi", [128, E * DC], BF16).ap()
        wrf = sbuf("wrf", [128, E * DC], F32).ap()
        cid_sb = sbuf("cid_sb", [1, 1], F32).ap()
        kconst = sbuf("kconst", [128, 2], F32).ap()
        dfTi = sbuf("dfTi", [64, 64], I32).ap()
        idx2i = sbuf("idx2i", [128, TC // 128], I32).ap()
        logbuf = AB3.ap()[:, 8448:8448 + 2 * NT * E].bitcast(F32)
        pload = sbuf("pload", [128, DH], BF16).ap()
        maskbig = sbuf("maskbig", [128, E * NT], BF16).ap()
        maskf = sbuf("maskf", [128, E * NT], F32).ap()
        eidx_mine = sbuf("eidx_mine", [128, NT], F32).ap()
        ptok_mine = sbuf("ptok_mine", [128, NT], F32).ap()
        lmax = sbuf("lmax", [128, NT], F32).ap()
        negl = sbuf("negl", [128, NT], F32).ap()
        sume = sbuf("sume", [128, NT], F32).ap()
        lg8 = sbuf("lg8", [128, E], F32).ap()
        mk8 = sbuf("mk8", [128, E], F32).ap()
        tmp8 = sbuf("tmp8", [128, E], F32).ap()
        srow = sbuf("srow", [128, 1], F32).ap()
        ssq = sbuf("ssq", [128, 1], F32).ap()
        mrow = sbuf("mrow", [128, 1], F32).ap()
        m2row = sbuf("m2row", [128, 1], F32).ap()
        vrow = sbuf("vrow", [128, 1], F32).ap()
        sqv = sbuf("sqv", [128, 1], F32).ap()
        rstd = sbuf("rstd", [128, 1], F32).ap()
        xnsumT = sbuf("xnsumT", [128, DC], F32).ap()
        xnsumb = sbuf("xnsumb", [128, DC], BF16).ap()
        negdT = sbuf("negdT", [128, 8], F32).ap()
        mxbuf = sbuf("mxbuf", [128, max(NTB * NP * 4, 128)], F32).ap()
        mxcol = sbuf("mxcol", [128, 1], F32).ap()
        mxone = sbuf("mxone", [1, 1], F32).ap()
        vsum_sb = sbuf("vsum_sb", [DH + 1, H], F32).ap()
        counts8 = sbuf("counts8", [1, E], F32).ap()
        eidxT = sbuf("eidxT", [NT, 128], F32).ap()
        ptokT = sbuf("ptokT", [NT, 128], F32).ap()
        mxall = sbuf("mxall", [NCORE, 1], F32).ap()
        mxsc = sbuf("mxsc", [1, 1], F32).ap()
        msca = sbuf("msca", [1, 1], F32).ap()
        mrep = sbuf("mrep", [128, 1], F32).ap()
        epsv = sbuf("epsv", [DH + 1, H], F32).ap()
        cols_sb = sbuf("cols_sb", [DH + 1, H], F32).ap()
        q2b = sbuf("q2b", [128, 512], BF16).ap()
        negdq = sbuf("negdq", [128, 2 * NT], F32).ap()
        rmx2 = sbuf("rmx2", [128, 2], F32).ap()
        biasq = sbuf("biasq", [128, 2], F32).ap()
        dinv2 = sbuf("dinv2", [128, 2], F32).ap()

        bc_reg = [None]

        def s0_g(g):
            bc_reg[0] = g.to_reg(T - 1)
            g.memset(identb[:], 0.0)
            g.memset(identf[:], 0.0)
            g.memset(Lb[:], 0.0)
            g.memset(ones128b[:], 1.0)
            g.memset(onescolf[:], 1.0)
            g.memset(onesrowb[:], 1.0)
            g.memset(onesrowf[:], 1.0)
            g.memset(ones2b[0:64, 0:1], 1.0)
            g.memset(ones2b[64:128, 0:1], 0.0)
            g.memset(ones2b[0:64, 1:2], 0.0)
            sGE.inc(g.memset(ones2b[64:128, 1:2], 1.0))
            sGE.wait(g)
            g.affine_select(out=identb[:], in_=identb[:], compare_op=ALU.not_equal,
                            fill=1.0, base=0, pattern=[[-1, 128]], channel_multiplier=1)
            g.affine_select(out=identf[:], in_=identf[:], compare_op=ALU.not_equal,
                            fill=1.0, base=0, pattern=[[-1, 128]], channel_multiplier=1)
            g.affine_select(out=Lb[:], in_=Lb[:], compare_op=ALU.is_ge,
                            fill=1.0, base=0, pattern=[[-1, 128]], channel_multiplier=1)
            g.memset(kconst[:, 0:1], LN_EPS)
            g.memset(kconst[:, 1:2], LNR)
            g.iota(e1000[:], pattern=[[1, E]], base=1000, channel_multiplier=0,
                   allow_small_or_imprecise_dtypes=True)
            g.iota(hiio[:], pattern=[[1, 64]], base=0, channel_multiplier=0,
                   allow_small_or_imprecise_dtypes=True)
            g.iota(piota[:], pattern=[[0, 1]], base=0, channel_multiplier=1,
                   allow_small_or_imprecise_dtypes=True)
            sGE.inc(g.iota(plo1[:], pattern=[[0, 1]], base=128, channel_multiplier=1,
                           allow_small_or_imprecise_dtypes=True))
            sWL.inc(g.dma_start(out=wrhi.rearrange("p (c e) -> p c e", e=E),
                                in_=wr_d.rearrange("(c p) e -> p c e", p=128)), 16)
            sWL.inc(g.dma_start(out=pload[:], in_=p_d[0:128, :]), 16)
        on("gpsimd", s0_g)
        GE_CONST = sGE.n

        def s0_s(s):
            sST.inc(s.dma_start(out=bo_row[:], in_=bo_d[:, :]), 16)
            sST.inc(s.dma_start(out=cid_sb[:], in_=cid_d[:, :]), 16)
            sST.inc(s.dma_start(out=wrf.rearrange("p (c e) -> p c e", e=E),
                                in_=wr_d.rearrange("(c p) e -> p c e", p=128)), 16)
        on("sync", s0_s)

        def s0_t1(t):
            t.wait_ge(sGE.h, GE_CONST)
            sWL.wait(t)
            t.transpose(bkb[6][0:64, 0:128], pload[:], identb[:])
            sBKP[6].inc(t.last)
        on("tensor", s0_t1)

        def s0_v1(v):
            sBKP[6].wait(v)
            v.memset(ptbd[:], 0.0)
            v.drain()
            v.tensor_scalar(ptbd[0:64, 0:128], bkb[6][0:64, 0:128], DN, None, ALU.mult)
            sBKC[6].inc(v.tensor_scalar(ptbd[64:128, M:M + 128], bkb[6][0:64, 0:128],
                                        DN, None, ALU.mult))
            sVE.inc(v.drain())
        on("vector", s0_v1)

        def s0_g2(g):
            sVE.wait(g)
            sWL.inc(g.dma_start(out=pload[:], in_=p_d[128:256, :]), 16)
        on("gpsimd", s0_g2)

        def s0_t2(t):
            sWL.wait(t)
            bfree(t, 6)
            t.transpose(bkb[6][0:64, 256:384], pload[:], identb[:])
            sBKP[6].inc(t.last)
            sST.wait(t)
            bfree(t, 7)
            t.matmul(bank[7][:, 0:1], onesrowf[:, :], cid_sb[:, :], start=True, stop=True)
            sBKP[7].inc(t.last)
        on("tensor", s0_t2)

        def s0_v2(v):
            sBKP[6].wait(v)
            sBKP[7].wait(v)
            v.tensor_scalar(ptbd[0:64, 128:256], bkb[6][0:64, 256:384], DN, None, ALU.mult)
            sBKC[6].inc(v.tensor_scalar(ptbd[64:128, M + 128:2 * M], bkb[6][0:64, 256:384],
                                        DN, None, ALU.mult))
            sBKC[7].inc(v.tensor_copy(cidrep[:], bank[7][:, 0:1]))
            v.drain()
            sWL.wait(v)
            for c in range(DC):
                v.tensor_copy(wrhl[:, c * 16:c * 16 + 8], wrhi[:, c * 8:(c + 1) * 8])
            v.drain()
            for c in range(DC):
                v.tensor_tensor(wrhl[:, c * 16 + 8:c * 16 + 16],
                                wrf[:, c * 8:(c + 1) * 8],
                                wrhl[:, c * 16:c * 16 + 8], ALU.subtract)
            sVE.inc(v.drain())
        on("vector", s0_v2)
        VE_CONST = sVE.n

        p0_xn_ve = [0] * NT
        p0_evac_ve = [0] * NT
        p0_rt_mark = [None] * NT
        p0_st_n = [0] * NT

        for tc in range(NT):
            par = tc % 2

            def p0_load(s, tc=tc, par=par):
                if tc >= 2:
                    s.wait_ge(sVE.h, p0_xn_ve[tc - 2])
                sXL[par].inc(s.dma_start(out=xbuf[par],
                                         in_=x_d[tc * 128:(tc + 1) * 128, :]), 16)
            on("sync", p0_load)

            def p0_v1(v, tc=tc, par=par):
                sXL[par].wait(v)
                if tc == 0:
                    v.wait_ge(sVE.h, VE_CONST)
                v.drain()
                sVE.inc(v.tensor_reduce(srow[:], xbuf[par], axis=AX.X, op=ALU.add))
            on("vector", p0_v1)

            def p0_a1(v, tc=tc, par=par):
                sXL[par].wait(v)
                v.drain()
                sVE.inc(v.scalar_tensor_tensor(xnfp, xbuf[par], 1.0, xbuf[par],
                                               ALU.mult, ALU.mult, accum_out=ssq[:]))
            on("vector", p0_a1)

            def p0_v2(v, tc=tc):
                sVE.wait(v)
                v.drain()
                v.tensor_scalar(mrow[:], srow[:], 1.0 / D, None, ALU.mult)
                v.drain()
                v.tensor_tensor(m2row[:], mrow[:], mrow[:], ALU.mult)
                v.tensor_scalar(vrow[:], ssq[:], 1.0 / D, None, ALU.mult)
                v.drain()
                sVE.inc(v.tensor_tensor(vrow[:], vrow[:], m2row[:], ALU.subtract))
            on("vector", p0_v2)

            def p0_a2(sc, tc=tc):
                sVE.wait(sc)
                sc.drain()
                sAE.inc(sc.activation(out=sqv[:], in_=vrow[:], func=AF.Sqrt,
                                      bias=kconst[:, 0:1], scale=1.0))
            on("scalar", p0_a2)

            def p0_v3(v, tc=tc, par=par):
                sAE.wait(v)
                if tc >= 2:
                    v.wait_ge(sXN.h, p0_st_n[tc - 1])
                    brt2, bkn2 = p0_rt_mark[tc - 2]
                    v.wait_ge(sBKP[brt2].h, bkn2)
                v.drain()
                v.reciprocal(rstd[:], sqv[:])
                v.drain()
                v.tensor_scalar(xnfp, xbuf[par], mrow[:], None, ALU.subtract)
                v.drain()
                v.tensor_scalar(xnfp, xnfp, rstd[:], None, ALU.mult)
                v.drain()
                v.tensor_copy(xnbf[par], xnfp)
                v.drain()
                sVE.inc(v.tensor_tensor(xloT[par], xnfp, xnbf[par], ALU.subtract))
                p0_xn_ve[tc] = sVE.n
            on("vector", p0_v3)

            def p0_st(s, tc=tc, par=par):
                s.wait_ge(sVE.h, p0_xn_ve[tc])
                sXN.inc(s.dma_start(out=xn_dram[tc * 128:(tc + 1) * 128, :],
                                    in_=xnbf[par]), 16)
                p0_st_n[tc] = sXN.n
            on("sync", p0_st)

            def p0_tr(t, tc=tc, par=par):
                t.wait_ge(sVE.h, p0_xn_ve[tc])
                bfree(t, 4)
                for c in range(DC):
                    t.transpose(bkb[4][:, c * 128:(c + 1) * 128],
                                xnbf[par][:, c * 128:(c + 1) * 128], identb[:])
                sBKP[4].inc(t.last)
                bfree(t, 6)
                for c in range(DC):
                    t.transpose(bkb[6][:, c * 128:(c + 1) * 128],
                                xloT[par][:, c * 128:(c + 1) * 128], identb[:])
                sBKP[6].inc(t.last)
            on("tensor", p0_tr)

            def p0_ev(v, tc=tc, par=par):
                sBKP[4].wait(v)
                sBKP[6].wait(v)
                v.drain()
                for c in range(DC):
                    v.tensor_copy(xnT[:, c * TC + tc * 128: c * TC + (tc + 1) * 128],
                                  bkb[4][:, c * 128:(c + 1) * 128])
                sBKC[4].inc(v.drain())
                for c in range(DC):
                    v.tensor_copy(xloT[par][:, c * 128:(c + 1) * 128],
                                  bkb[6][:, c * 128:(c + 1) * 128])
                sBKC[6].inc(v.drain())
                sVE.inc(v.drain())
                p0_evac_ve[tc] = sVE.n
            on("vector", p0_ev)

            brt = 5 if tc % 2 == 0 else 7
            bra = 3 if tc % 2 == 0 else 2

            def p0_rt(t, tc=tc, par=par, brt=brt, bra=bra):
                t.wait_ge(sVE.h, p0_evac_ve[tc])
                bfree(t, brt)
                bfree(t, bra)
                for c in range(DC):
                    t.matmul(bank[brt][:, 0:16],
                             xnT[:, c * TC + tc * 128: c * TC + (tc + 1) * 128],
                             wrhl[:, c * 16:(c + 1) * 16],
                             start=(c == 0), stop=(c == DC - 1))
                sBKP[brt].inc(t.last)
                for c in range(DC):
                    t.matmul(bank[bra][:, 32:40],
                             xloT[par][:, c * 128:(c + 1) * 128],
                             wrhi[:, c * 8:(c + 1) * 8],
                             start=(c == 0), stop=(c == DC - 1))
                sBKP[bra].inc(t.last)
                p0_rt_mark[tc] = (bra, sBKP[bra].n)
            on("tensor", p0_rt)

            def p0_rl(v, tc=tc, brt=brt, bra=bra):
                sBKP[brt].wait(v)
                sBKP[bra].wait(v)
                v.drain()
                v.tensor_copy(lg8[:], bank[brt][:, 0:8])
                v.drain()
                sBKC[brt].inc(v.tensor_tensor(lg8[:], lg8[:], bank[brt][:, 8:16], ALU.add))
                v.drain()
                sBKC[bra].inc(v.tensor_tensor(logbuf[:, tc * E:(tc + 1) * E],
                                              lg8[:], bank[bra][:, 32:40], ALU.add))
                v.drain()
                v.tensor_reduce(lmax[:, tc:tc + 1], logbuf[:, tc * E:(tc + 1) * E],
                                axis=AX.X, op=ALU.max)
                v.drain()
                v.tensor_scalar(mk8[:], logbuf[:, tc * E:(tc + 1) * E],
                                lmax[:, tc:tc + 1], None, ALU.is_ge)
                v.drain()
                v.scalar_tensor_tensor(tmp8[:], mk8[:], -1000.0, e1000[:],
                                       ALU.mult, ALU.add)
                v.drain()
                sVE.inc(v.tensor_reduce(eidx_mine[:, tc:tc + 1], tmp8[:],
                                        axis=AX.X, op=ALU.min))
            on("vector", p0_rl)

        def p0_fv(v):
            v.drain()
            sVE.inc(v.tensor_scalar(negl[:], lmax[:], -1.0, None, ALU.mult))
        on("vector", p0_fv)
        NEGL_VE = sVE.n

        def p0_fa(sc):
            sc.wait_ge(sVE.h, NEGL_VE)
            sc.drain()
            for tc in range(NT):
                sc.activation(out=logbuf[:, tc * E:(tc + 1) * E],
                              in_=logbuf[:, tc * E:(tc + 1) * E], func=AF.Exp,
                              bias=negl[:, tc:tc + 1], scale=1.0,
                              accum_out=sume[:, tc:tc + 1])
                sc.drain()
            sAE.inc(sc.drain())
        on("scalar", p0_fa)

        def p0_fv2(v):
            sAE.wait(v)
            v.drain()
            v.reciprocal(ptok_mine[:], sume[:])
            v.drain()
            for e in range(E):
                v.tensor_scalar(maskf[:, e * NT:(e + 1) * NT], eidx_mine[:],
                                float(e), None, ALU.is_equal)
            v.drain()
            sVE.inc(v.tensor_copy(maskbig[:], maskf[:]))
        on("vector", p0_fv2)
        P0_VE = sVE.n

        if debug >= 1:
            def dbg_p0(s):
                s.wait_ge(sVE.h, P0_VE)
                sDBG.inc(s.dma_start(out=dbg["eidx"][:, :], in_=eidx_mine[:]), 16)
                sDBG.inc(s.dma_start(out=dbg["ptok"][:, :], in_=ptok_mine[:]), 16)
            on("sync", dbg_p0)

            def dbg_p0b(g):
                g.wait_ge(sXN.h, sXN.n)
                sDBG2.inc(g.dma_start(out=dbg["xn"][:, :], in_=xn_dram[:, :]), 16)
            on("gpsimd", dbg_p0b)


        def _stub_final():
            def zv(v):
                v.drain()
                sVE.inc(v.memset(y1buf[0], 0.0))
            on("vector", zv)
            zve = sVE.n

            def zs(s):
                s.wait_ge(sVE.h, zve)
                for tc2 in range(NT):
                    sOUT.inc(s.dma_start(out=y_d[tc2 * 128:(tc2 + 1) * 128, :],
                                         in_=y1buf[0]), 16)
                s.wait_ge(sOUT.h, sOUT.n)
            on("sync", zs)

        def ag_xn(g):
            g.wait_ge(sXN.h, sXN.n)
            sCC1.inc(g.collective_compute(
                "AllGather", ALU.bypass, replica_groups=[list(range(NCORE))],
                ins=[xn_dram.opt()], outs=[xnall.opt()]))
        on("gpsimd", ag_xn)
        if trunc == 1:
            _stub_final()
            return nc

        def a_wld(g):
            sWL.inc(g.dma_start(out=wv_sb.rearrange("p (c j) -> p c j", j=D),
                                in_=wv_d.rearrange("(c p) j -> p c j", p=128)), 16)
            sWL.inc(g.dma_start(out=wkfull.rearrange("p (c j) -> p c j", j=D),
                                in_=wk_d.rearrange("(c p) j -> p c j", p=128)), 16)
        on("gpsimd", a_wld)
        WL_A = sWL.n

        def a_vpo(g):
            for i in range(2):
                g.memset(vp[i][:, 64::65], 1.0)
            sGE.inc(g.last)
        on("gpsimd", a_vpo)
        GE_VPO = sGE.n

        kp_te_free = [0, 0]
        ctx_mark = [0]
        vp_te_mark = [0, 0]
        kev_gate_te = [None, None]
        ksq_gate_te = None
        dg_gate_ae = 0

        for tb in range(NTB):
            par = tb % 2

            for sub in range(4):
                tcg = tb * 4 + sub

                def a_vmm(t, tb=tb, sub=sub, tcg=tcg, par=par):
                    if tb == 0 and sub == 0:
                        t.wait_ge(sWL.h, WL_A)
                        t.wait_ge(sGE.h, GE_VPO)
                    t.wait_ge(sVE.h, p0_evac_ve[tcg])
                    if tb >= 2 and sub == 0:
                        t.wait_ge(sTE.h, vp_te_mark[par])
                    bfree(t, 0)
                    bfree(t, 1)
                    for c in range(DC):
                        t.matmul(bank[0][:, :],
                                 xnT[:, c * TC + tcg * 128: c * TC + (tcg + 1) * 128],
                                 wv_sb[:, c * D: c * D + 512],
                                 start=(c == 0), stop=(c == DC - 1))
                    sBKP[0].inc(t.last)
                    for c in range(DC):
                        t.matmul(bank[1][:, :],
                                 xnT[:, c * TC + tcg * 128: c * TC + (tcg + 1) * 128],
                                 wv_sb[:, c * D + 512: c * D + 1024],
                                 start=(c == 0), stop=(c == DC - 1))
                    sBKP[1].inc(t.last)
                on("tensor", a_vmm)

                def a_vev(v, sub=sub, par=par):
                    sBKP[0].wait(v)
                    sBKP[1].wait(v)
                    v.drain()
                    for h in range(H):
                        src = bank[0] if h < 8 else bank[1]
                        off = (h % 8) * 64
                        v.tensor_copy(vp[par][:, (sub * H + h) * 65:(sub * H + h) * 65 + 64],
                                      src[:, off:off + 64])
                    sBKC[0].inc(v.drain())
                    sBKC[1].inc(v.drain())
                    sVE.inc(v.drain())
                on("vector", a_vev)
            vp_ready = sVE.n

            for pair in range(NP):
                if trunc == 11:
                    continue
                kpar = pair % 2
                bc0, bc1 = (4, 6) if pair % 2 == 0 else (5, 7)
                bcap0 = bkf[4] if pair % 2 == 0 else bank[5]
                bcap1 = bkf[6] if pair % 2 == 0 else bank[7]

                def a_kmm(t, tb=tb, pair=pair):
                    bfree(t, 2)
                    for c in range(DC):
                        t.matmul(bank[2][:, :],
                                 wkfull[:, c * D + pair * 128: c * D + (pair + 1) * 128],
                                 xnT[:, c * TC + tb * 512: c * TC + (tb + 1) * 512],
                                 start=(c == 0), stop=(c == DC - 1))
                    sBKP[2].inc(t.last)
                on("tensor", a_kmm)

                if True:
                    def a_kev(v, kpar=kpar, pair=pair):
                        sBKP[2].wait(v)
                        if kev_gate_te[kpar]:
                            v.wait_ge(*kev_gate_te[kpar])
                        v.drain()
                        sVE.inc(v.tensor_copy(kblk[kpar], bank[2][:, :]))
                    on("vector", a_kev)
                kev_ve = sVE.n

                if True:
                    def a_ksq(v, tb=tb, pair=pair, kpar=kpar, kev_ve=kev_ve):
                        if ksq_gate_te:
                            v.wait_ge(*ksq_gate_te)
                        v.drain()
                        sVE.inc(v.scalar_tensor_tensor(k2b[:], kblk[kpar], DN * DN,
                                                       kblk[kpar], ALU.mult, ALU.mult))
                    on("vector", a_ksq)
                ksq_ae = sVE.n

                if trunc in (121, 1211, 1212, 1213):
                    def a_rel(t, ksq_ae=ksq_ae, kev_ve=kev_ve):
                        sAE.wait(t, ksq_ae)
                        t.wait_ge(sVE.h, kev_ve)
                        sBKC[2].inc(t.nop())
                    on("tensor", a_rel)
                    continue

                def a_dgm(t, ksq_ae=ksq_ae, kev_ve=kev_ve):
                    t.wait_ge(sVE.h, ksq_ae)
                    t.wait_ge(sVE.h, kev_ve)
                    sBKC[2].inc(t.nop())
                    bfree(t, 3)
                    t.matmul(bank[3][0:2, 0:512], ones2b[:, :], k2b[:], start=True, stop=True)
                    sBKP[3].inc(t.last)
                on("tensor", a_dgm)
                ksq_gate_te = (sBKP[3].h, sBKP[3].n)

                def a_dge(v, dg_gate_ae=dg_gate_ae):
                    sBKP[3].wait(v)
                    if dg_gate_ae:
                        v.wait_ge(sAE.h, dg_gate_ae)
                    v.drain()
                    sBKC[3].inc(v.tensor_scalar(diagp[:], bank[3][0:2, 0:512],
                                                0.5, None, ALU.mult))
                    sVE.inc(v.drain())
                on("vector", a_dge)
                dge_ve = sVE.n

                def a_dgt(t, dge_ve=dge_ve):
                    t.wait_ge(sVE.h, dge_ve)
                    bfree(t, 3)
                    for sub in range(4):
                        t.transpose(bank[3][:, sub * 2: sub * 2 + 2],
                                    diagp[:, sub * 128:(sub + 1) * 128], identf[0:2, 0:2])
                    sBKP[3].inc(t.last)
                on("tensor", a_dgt)

                def a_dg2(v):
                    sBKP[3].wait(v)
                    v.drain()
                    sBKC[3].inc(v.tensor_scalar(negdT[:], bank[3][:, 0:8],
                                                -1.0, None, ALU.mult))
                    sVE.inc(v.drain())
                on("vector", a_dg2)
                negd_ve = sVE.n

                if trunc == 12:
                    def a_rel2(v):
                        sVE.inc(v.drain())
                    on("vector", a_rel2)
                    continue
                for sub in range(4):
                    bdd = 3 if sub % 2 == 0 else 2
                    kpp = sub % 2

                    def a_dd(t, sub=sub, bdd=bdd, kpar=kpar, kev_ve=kev_ve):
                        t.wait_ge(sVE.h, kev_ve)
                        bfree(t, bdd)
                        t.matmul(bank[bdd][:, 0:512],
                                 kblk[kpar][:, sub * 128:(sub + 1) * 128],
                                 ptbd[:, :], start=True, stop=True)
                        sBKP[bdd].inc(t.last)
                    on("tensor", a_dd)

                    if trunc in (131, 133):
                        def a_mxstub(v, bdd=bdd):
                            sBKP[bdd].wait(v)
                            sBKC[bdd].inc(v.drain())
                            sVE.inc(v.drain())
                        on("vector", a_mxstub)
                        mx_ve = sVE.n
                    else:
                        pass

                    def a_mx(v, tb=tb, pair=pair, sub=sub, bdd=bdd):
                        sBKP[bdd].wait(v)
                        col = (tb * NP + pair) * 4 + sub
                        v.drain()
                        sBKC[bdd].inc(v.tensor_reduce(mxbuf[:, col:col + 1],
                                                      bank[bdd][:, :], axis=AX.X, op=ALU.max))
                        sVE.inc(v.drain())
                    if trunc not in (131, 133):
                        on("vector", a_mx)
                        mx_ve = sVE.n

                    if trunc in (131, 132):
                        def a_exstub(sc, bdd=bdd, mx_ve=mx_ve, kpp=kpp):
                            sc.wait_ge(sVE.h, mx_ve)
                            sc.wait_ge(sTE.h, kp_te_free[kpp])
                            sc.drain()
                            sBKC[bdd].inc(sc.drain())
                            sAE.inc(sc.drain())
                        on("scalar", a_exstub)
                        exp_ae = sAE.n
                        if sub == 3:
                            dg_gate_ae = exp_ae

                    def a_ex(sc, sub=sub, bdd=bdd, kpp=kpp, negd_ve=negd_ve,
                             mx_ve=mx_ve):
                        sBKP[bdd].wait(sc)
                        sc.wait_ge(sVE.h, mx_ve)
                        sc.wait_ge(sVE.h, negd_ve)
                        sc.wait_ge(sTE.h, kp_te_free[kpp])
                        sc.drain()
                        sc.activation(out=kp_sb[kpp][:, 0:256], in_=bank[bdd][:, 0:256],
                                      func=AF.Exp, bias=negdT[:, sub * 2: sub * 2 + 1],
                                      scale=1.0)
                        sBKC[bdd].inc(sc.activation(
                            out=kp_sb[kpp][:, 256:512], in_=bank[bdd][:, 256:512],
                            func=AF.Exp, bias=negdT[:, sub * 2 + 1: sub * 2 + 2],
                            scale=1.0))
                        sAE.inc(sc.drain())
                    if trunc not in (131, 132):
                        on("scalar", a_ex)
                        exp_ae = sAE.n
                        if sub == 3:
                            dg_gate_ae = exp_ae

                    if trunc in (13, 131, 132, 133):
                        kp_te_free[kpp] = sTE.n
                        if sub == 3:
                            ctx_mark[0] = sTE.n
                        continue

                    def a_m3(t, tb=tb, pair=pair, sub=sub, par=par, kpp=kpp,
                             exp_ae=exp_ae, bc0=bc0, bc1=bc1, bcap0=bcap0,
                             bcap1=bcap1, vp_ready=vp_ready):
                        sAE.wait(t, exp_ae)
                        t.wait_ge(sVE.h, vp_ready)
                        if sub == 0:
                            bfree(t, bc0)
                            bfree(t, bc1)
                        for h2 in range(2):
                            h = pair * 2 + h2
                            cap = bcap0 if h2 == 0 else bcap1
                            t.matmul(cap[0:DH + 1, 0:256],
                                     vp[par][:, (sub * H + h) * 65:(sub * H + h + 1) * 65],
                                     kp_sb[kpp][:, h2 * 256:(h2 + 1) * 256],
                                     start=(sub == 0), stop=(sub == 3))
                        sTE.inc(t.last)
                        kp_te_free[kpp] = sTE.n
                        if sub == 3:
                            ctx_mark[0] = sTE.n
                    on("tensor", a_m3)

                kev_gate_te[kpar] = (sBKP[2].h, sBKP[2].n)

                if trunc in (13, 131, 132, 133):
                    continue
                cm = ctx_mark[0]

                def a_ctx(v, tb=tb, pair=pair, bc0=bc0, bc1=bc1, bcap0=bcap0,
                          bcap1=bcap1, cm=cm):
                    v.wait_ge(sTE.h, cm)
                    v.drain()
                    for h2 in range(2):
                        cap = bcap0 if h2 == 0 else bcap1
                        bci = bc0 if h2 == 0 else bc1
                        dst = ctx_acc[:, (pair * 2 + h2) * 256:(pair * 2 + h2 + 1) * 256]
                        if tb == 0:
                            sBKC[bci].inc(v.tensor_copy(dst, cap[0:DH + 1, 0:256]))
                        else:
                            sBKC[bci].inc(v.tensor_tensor(dst, dst,
                                                          cap[0:DH + 1, 0:256], ALU.add))
                        v.drain()
                    sVE.inc(v.drain())
                on("vector", a_ctx)

            vp_te_mark[par] = sTE.n
        def a_bar(t):
            sTE.inc(t.drain())
        on("tensor", a_bar)
        A_BAR = sTE.n
        if trunc in (11, 12, 121, 1211, 1212, 1213, 13, 131, 132, 133, 14):
            _stub_final()
            return nc
        A_VE = sVE.n

        def a_mxf(v):
            v.wait_ge(sVE.h, A_VE)
            v.drain()
            sVE.inc(v.tensor_reduce(mxcol[:], mxbuf[:, 0:NTB * NP * 4],
                                    axis=AX.X, op=ALU.max))
        on("vector", a_mxf)
        MXC_VE = sVE.n

        def a_mxt(t):
            t.wait_ge(sVE.h, MXC_VE)
            bfree(t, 3)
            t.transpose(bank[3][0:1, 16:144], mxcol[:], identf[:])
            sBKP[3].inc(t.last)
        on("tensor", a_mxt)

        def a_mxv(v):
            sBKP[3].wait(v)
            v.drain()
            sBKC[3].inc(v.tensor_reduce(mxone[:], bank[3][0:1, 16:144],
                                        axis=AX.X, op=ALU.max))
            sVE.inc(v.drain())
        on("vector", a_mxv)
        MXONE_VE = sVE.n

        def a_xns(v):
            v.drain()
            for c in range(DC):
                v.tensor_reduce(xnsumT[:, c:c + 1], xnT[:, c * TC:(c + 1) * TC],
                                axis=AX.X, op=ALU.add)
            v.drain()
            sVE.inc(v.tensor_copy(xnsumb[:], xnsumT[:]))
        on("vector", a_xns)
        XNS_VE = sVE.n

        def a_vsm(t):
            t.wait_ge(sVE.h, XNS_VE)
            bfree(t, 3)
            for vcc in range(DC):
                for c in range(DC):
                    t.matmul(bank[3][:, 32 + vcc: 32 + vcc + 1],
                             wv_sb[:, c * D + vcc * 128: c * D + vcc * 128 + 128],
                             xnsumb[:, c:c + 1],
                             start=(c == 0), stop=(c == DC - 1))
            sBKP[3].inc(t.last)
        on("tensor", a_vsm)
        A_TE = (sBKP[3].h, sBKP[3].n)

        def a_vse(v):
            sBKP[3].wait(v)
            v.drain()
            for h in range(H):
                vcc = h // 2
                base = (h % 2) * 64
                v.tensor_copy(vsum_sb[0:64, h:h + 1],
                              bank[3][base:base + 64, 32 + vcc:32 + vcc + 1])
            sBKC[3].inc(v.drain())
            v.memset(vsum_sb[64:65, :], float(TC))
            sVE.inc(v.drain())
        on("vector", a_vse)
        VSUM_VE = sVE.n

        def b_cm(t):
            t.wait_ge(sVE.h, P0_VE)
            bfree(t, 5)
            t.matmul(bank[5][0:1, 0:E * NT], ones128b[:, :], maskbig[:], start=True, stop=True)
            sBKP[5].inc(t.last)
            bfree(t, 7)
            t.transpose(bank[7][0:NT, 0:128], eidx_mine[:], identf[:])
            t.transpose(bank[7][0:NT, 128:256], ptok_mine[:], identf[:])
            sBKP[7].inc(t.last)
        on("tensor", b_cm)

        def b_ce(v):
            sBKP[5].wait(v)
            sBKP[7].wait(v)
            v.drain()
            sBKC[5].inc(v.tensor_reduce(
                counts8[:],
                bank[5][0:1, 0:E * NT].rearrange("a (e t) -> a e t", t=NT),
                axis=AX.X, op=ALU.add))
            v.tensor_copy(eidxT[:], bank[7][0:NT, 0:128])
            sBKC[7].inc(v.tensor_copy(ptokT[:], bank[7][0:NT, 128:256]))
            sVE.inc(v.drain())
        on("vector", b_ce)
        MSGB_VE = sVE.n

        def msg_st(s):
            s.wait_ge(sVE.h, A_VE)
            sMSG.inc(s.dma_start(out=msgA_in[:, 0:H * M], in_=ctx_acc[:, :]), 16)
            s.wait_ge(sVE.h, VSUM_VE)
            sMSG.inc(s.dma_start(out=msgA_in[:, H * M:], in_=vsum_sb[:, :]), 16)
            s.wait_ge(sVE.h, MXONE_VE)
            s.wait_ge(sVE.h, MSGB_VE)
            sMSG.inc(s.dma_start(out=msgB_in[:, 0:1], in_=mxone[:, :]), 16)
            sMSG.inc(s.dma_start(out=msgB_in[:, 1:1 + E], in_=counts8[:, :]), 16)
            sMSG.inc(s.dma_start(out=msgB_in[:, 1 + E:16], in_=e1000[0:1, 0:15 - E]), 16)
            sMSG.inc(s.dma_start(
                out=msgB_in[0:1, 16:16 + TC].rearrange("a (t p) -> (a t) p", p=128),
                in_=eidxT[:, :]), 16)
            sMSG.inc(s.dma_start(
                out=msgB_in[0:1, 16 + TC:16 + 2 * TC].rearrange("a (t p) -> (a t) p", p=128),
                in_=ptokT[:, :]), 16)
        on("sync", msg_st)

        def msg_cc(g):
            g.wait_ge(sMSG.h, sMSG.n)
            sCC2.inc(g.collective_compute(
                "AllReduce", ALU.add,
                replica_groups=[[0, 1], [2, 3], [4, 5], [6, 7]],
                ins=[msgA_in.opt()], outs=[msgA_out.opt()]))
            sCC3.inc(g.collective_compute(
                "AllGather", ALU.bypass, replica_groups=[list(range(NCORE))],
                ins=[msgB_in.opt()], outs=[msgB_out.opt()]))
        on("gpsimd", msg_cc)

        if trunc == 2:
            _stub_final()
            return nc
        if debug >= 2:
            def dbg_a(s):
                sCC2.wait(s)
                sDBG.inc(s.dma_start(out=dbg["ctx"][:, :], in_=msgA_out[:, :]), 16)
            on("sync", dbg_a)

        def cc_ld(s):
            sCC2.wait(s)
            sCC3.wait(s)
            sMGL.inc(s.dma_start(out=ctxF[:, 0:H * M], in_=msgA_out[:, 0:H * M]), 16)
            sMGL.inc(s.dma_start(out=vsum_sb[:, :], in_=msgA_out[:, H * M:]), 16)
            sMGL.inc(s.dma_start(out=mxall[:, :], in_=msgB_out[:, 0:1],
                                 allow_slow_non_contiguous=True), 16)
        on("sync", cc_ld)
        CCLD_M = sMGL.n

        def cc_t1(t):
            t.wait_ge(sMGL.h, CCLD_M)
            bfree(t, 7)
            t.transpose(bank[7][0:1, 0:NCORE], mxall[:, :], identf[0:NCORE, 0:NCORE])
            sBKP[7].inc(t.last)
        on("tensor", cc_t1)

        def cc_v1(v):
            sBKP[7].wait(v)
            v.drain()
            sBKC[7].inc(v.tensor_reduce(mxsc[:], bank[7][0:1, 0:NCORE],
                                        axis=AX.X, op=ALU.max))
            sVE.inc(v.drain())
        on("vector", cc_v1)
        MXS_VE = sVE.n

        if debug >= 2:
            def dbg_mx(s):
                s.wait_ge(sVE.h, MXS_VE)
                sDBG.inc(s.dma_start(out=dbg["mx"][:, :], in_=mxsc[:, :]), 16)
            on("sync", dbg_mx)

        def cc_a1(sc):
            sc.wait_ge(sVE.h, MXS_VE)
            sc.drain()
            sAE.inc(sc.activation(out=msca[:], in_=mxsc[:], func=AF.Exp,
                                  bias=kconst[0:1, 1:2], scale=-1.0))
        on("scalar", cc_a1)
        MSCA_AE = sAE.n

        def cc_t2(t):
            t.wait_ge(sAE.h, MSCA_AE)
            bfree(t, 7)
            t.matmul(bank[7][:, 20:21], onesrowf[:, :], msca[:, :], start=True, stop=True)
            sBKP[7].inc(t.last)
        on("tensor", cc_t2)

        def cc_v2(v):
            sBKP[7].wait(v)
            v.drain()
            sBKC[7].inc(v.tensor_copy(mrep[:], bank[7][:, 20:21]))
            v.drain()
            v.tensor_scalar(epsv[:, :], vsum_sb[:, :], RATIO * EPS_PERF, None, ALU.mult)
            v.drain()
            for h in range(H):
                v.tensor_scalar(ctxF[:, h * M:(h + 1) * M], ctxF[:, h * M:(h + 1) * M],
                                mrep[0:DH + 1, 0:1], None, ALU.mult)
            v.drain()
            for h in range(H):
                v.tensor_scalar(ctxF[:, h * M:(h + 1) * M], ctxF[:, h * M:(h + 1) * M],
                                epsv[:, h:h + 1], None, ALU.add)
            v.drain()
            for h in range(H):
                v.tensor_reduce(cols_sb[:, h:h + 1], ctxF[:, h * M:(h + 1) * M],
                                axis=AX.X, op=ALU.add)
            v.drain()
            sVE.inc(v.tensor_scalar(cols_sb[:, :], cols_sb[:, :], RATIO * EPS_PERF,
                                    None, ALU.mult))
        on("vector", cc_v2)
        CMB_VE = sVE.n

        for h in range(H):
            def cc_tr(t, h=h):
                if h == 0:
                    t.wait_ge(sVE.h, CMB_VE)
                bfree(t, 7)
                for mc in range(NMC):
                    t.transpose(bank[7][:, mc * 80: mc * 80 + 65],
                                ctxF[0:DH + 1, h * M + mc * 128: h * M + (mc + 1) * 128],
                                identf[0:DH + 1, 0:DH + 1])
                t.transpose(bank[7][0:1, 400:465], cols_sb[0:DH + 1, h:h + 1], identf[0:DH + 1, 0:DH + 1])
                sBKP[7].inc(t.last)
            on("tensor", cc_tr)

            def cc_te(v, h=h):
                sBKP[7].wait(v)
                v.drain()
                for mc in range(NMC):
                    v.tensor_copy(ctxT[:, (h * NMC + mc) * 65:(h * NMC + mc + 1) * 65],
                                  bank[7][:, mc * 80: mc * 80 + 65])
                sBKC[7].inc(v.tensor_copy(colrow[:, h * 65:(h + 1) * 65],
                                          bank[7][0:1, 400:465]))
                sVE.inc(v.drain())
            on("vector", cc_te)
        CTX_VE = sVE.n

        wq_te_free = [None, None]
        qp_te_free = [None, None]
        hp_te_free = [None]
        q2_gate_te = [None]
        dq_gate_ae = [0]
        m4s_prev_ae = [0]

        def b_wq01(g):
            g.wait_ge(*A_TE)
            for pair in range(2):
                sWQ[pair].inc(g.dma_start(
                    out=wq_sb[pair].rearrange("p (c j) -> p c j", j=128),
                    in_=wq_d[:, pair * 128:(pair + 1) * 128].rearrange(
                        "(c p) j -> p c j", p=128)), 16)
        on("gpsimd", b_wq01)

        for pair in range(NP):
            par = pair % 2

            if pair >= 2:
                def b_wq(g, pair=pair, par=par):
                    g.wait_ge(*wq_te_free[par])
                    sWQ[par].inc(g.dma_start(
                        out=wq_sb[par].rearrange("p (c j) -> p c j", j=128),
                        in_=wq_d[:, pair * 128:(pair + 1) * 128].rearrange(
                            "(c p) j -> p c j", p=128)), 16)
                on("gpsimd", b_wq)
            wq_wl = sWQ[par].n

            for ft in range(NTB):
                bq = ft % 2

                def b_qmm(t, pair=pair, par=par, ft=ft, bq=bq, wq_wl=wq_wl):
                    if ft == 0:
                        t.wait_ge(sWQ[par].h, wq_wl)
                        if pair == 0:
                            t.wait_ge(sVE.h, CTX_VE)
                    bfree(t, bq)
                    for c in range(DC):
                        t.matmul(bank[bq][:, :],
                                 wq_sb[par][:, c * 128:(c + 1) * 128],
                                 xnT[:, c * TC + ft * 512: c * TC + (ft + 1) * 512],
                                 start=(c == 0), stop=(c == DC - 1))
                    sBKP[bq].inc(t.last)
                on("tensor", b_qmm)

                def b_qev(v, ft=ft, bq=bq):
                    sBKP[bq].wait(v)
                    v.drain()
                    sVE.inc(v.tensor_copy(qblk[:, ft * 512:(ft + 1) * 512], bank[bq][:, :]))
                on("vector", b_qev)
                qev_ve = sVE.n

                def b_qsq(v, ft=ft, bq=bq, qev_ve=qev_ve):
                    if q2_gate_te[0]:
                        v.wait_ge(*q2_gate_te[0])
                    v.drain()
                    sVE.inc(v.scalar_tensor_tensor(q2b[:], qblk[:, ft * 512:(ft + 1) * 512],
                                                   DN * DN, qblk[:, ft * 512:(ft + 1) * 512],
                                                   ALU.mult, ALU.mult))
                on("vector", b_qsq)
                qsq_ae = sVE.n

                def b_qdg(t, bq=bq, qsq_ae=qsq_ae, qev_ve=qev_ve):
                    t.wait_ge(sVE.h, qsq_ae)
                    t.wait_ge(sVE.h, qev_ve)
                    sBKC[bq].inc(t.nop())
                    bfree(t, 3)
                    t.matmul(bank[3][0:2, 0:512], ones2b[:, :], q2b[:], start=True, stop=True)
                    sBKP[3].inc(t.last)
                    q2_gate_te[0] = (sBKP[3].h, sBKP[3].n)
                on("tensor", b_qdg)

                def b_qde(v):
                    sBKP[3].wait(v)
                    if dq_gate_ae[0]:
                        v.wait_ge(sAE.h, dq_gate_ae[0])
                    v.drain()
                    sBKC[3].inc(v.tensor_scalar(diagp[:], bank[3][0:2, 0:512],
                                                0.5, None, ALU.mult))
                    sVE.inc(v.drain())
                on("vector", b_qde)
                qde_ve = sVE.n

                def b_qdt(t, qde_ve=qde_ve):
                    t.wait_ge(sVE.h, qde_ve)
                    bfree(t, 3)
                    for sub in range(4):
                        t.transpose(bank[3][:, sub * 2: sub * 2 + 2],
                                    diagp[:, sub * 128:(sub + 1) * 128], identf[0:2, 0:2])
                    sBKP[3].inc(t.last)
                on("tensor", b_qdt)

                def b_qd2(v, ft=ft):
                    sBKP[3].wait(v)
                    v.drain()
                    sBKC[3].inc(v.tensor_scalar(negdq[:, ft * 8:(ft + 1) * 8],
                                                bank[3][:, 0:8], -1.0, None, ALU.mult))
                    sVE.inc(v.drain())
                on("vector", b_qd2)

            def b_ndl(v):
                v.drain()
                sVE.inc(v.tensor_scalar(negdq[:, :], negdq[:, :], 1.0, LNR,
                                        ALU.mult, ALU.add))
            on("vector", b_ndl)
            ndl_ve = sVE.n

            prev_exp_ae = [0]
            for tc in range(NT):
                bdd = 2 if tc % 2 == 0 else 3
                qpp = tc % 2

                def b_dq(t, tc=tc, bdd=bdd, ndl_ve=ndl_ve):
                    t.wait_ge(sVE.h, ndl_ve)
                    bfree(t, bdd)
                    t.matmul(bank[bdd][:, 0:512],
                             qblk[:, tc * 128:(tc + 1) * 128],
                             ptbd[:, :], start=True, stop=True)
                    sBKP[bdd].inc(t.last)
                on("tensor", b_dq)

                def b_rmx(v, tc=tc, bdd=bdd):
                    sBKP[bdd].wait(v)
                    if prev_exp_ae[0]:
                        v.wait_ge(sAE.h, prev_exp_ae[0])
                    v.drain()
                    v.tensor_reduce(rmx2[:, 0:1], bank[bdd][:, 0:256], axis=AX.X, op=ALU.max)
                    v.tensor_reduce(rmx2[:, 1:2], bank[bdd][:, 256:512], axis=AX.X, op=ALU.max)
                    v.drain()
                    sVE.inc(v.tensor_tensor(biasq[:, :], negdq[:, tc * 2:(tc + 1) * 2],
                                            rmx2[:, :], ALU.subtract))
                on("vector", b_rmx)
                rmx_ve = sVE.n

                def b_ex(sc, tc=tc, bdd=bdd, qpp=qpp, rmx_ve=rmx_ve):
                    sc.wait_ge(sVE.h, rmx_ve)
                    if qp_te_free[qpp]:
                        sc.wait_ge(*qp_te_free[qpp])
                    sc.drain()
                    sc.activation(out=qp_sb[qpp][:, 0:256], in_=bank[bdd][:, 0:256],
                                  func=AF.Exp, bias=biasq[:, 0:1], scale=1.0)
                    sBKC[bdd].inc(sc.activation(
                        out=qp_sb[qpp][:, 256:512], in_=bank[bdd][:, 256:512],
                        func=AF.Exp, bias=biasq[:, 1:2], scale=1.0))
                    sAE.inc(sc.drain())
                on("scalar", b_ex)
                exp_ae = sAE.n
                prev_exp_ae[0] = exp_ae
                if tc == NT - 1:
                    dq_gate_ae[0] = exp_ae

                def b_qtr(t, qpp=qpp, exp_ae=exp_ae):
                    sAE.wait(t, exp_ae)
                    bfree(t, 4)
                    for j in range(4):
                        t.transpose(bkb[4][:, j * 128:(j + 1) * 128],
                                    qp_sb[qpp][:, j * 128:(j + 1) * 128], identb[:])
                    sBKP[4].inc(t.last)
                    qp_te_free[qpp] = (sBKP[4].h, sBKP[4].n)
                on("tensor", b_qtr)

                def b_qte(v, qpp=qpp):
                    sBKP[4].wait(v)
                    v.drain()
                    sBKC[4].inc(v.tensor_copy(qpT[qpp], bkb[4][:, 0:512]))
                    sVE.inc(v.drain())
                on("vector", b_qte)
                qte_ve = sVE.n

                def b_m4(t, pair=pair, tc=tc, qpp=qpp, qte_ve=qte_ve):
                    t.wait_ge(sVE.h, qte_ve)
                    bfree(t, 5)
                    bfree(t, 7)
                    for h2 in range(2):
                        h = pair * 2 + h2
                        bm = bank[5] if h2 == 0 else bank[7]
                        for mc in range(NMC):
                            t.matmul(bm[:, 0:65],
                                     qpT[qpp][:, (h2 * NMC + mc) * 128:(h2 * NMC + mc + 1) * 128],
                                     ctxT[:, (h * NMC + mc) * 65:(h * NMC + mc + 1) * 65],
                                     start=(mc == 0), stop=False)
                        t.matmul(bm[:, 0:65],
                                 onesrowb[0:1, 0:128],
                                 colrow[:, h * 65:(h + 1) * 65],
                                 start=False, stop=True)
                        if h2 == 0:
                            sBKP[5].inc(t.last)
                        else:
                            sBKP[7].inc(t.last)
                on("tensor", b_m4)

                def b_m4r(v):
                    sBKP[5].wait(v)
                    sBKP[7].wait(v)
                    v.drain()
                    v.reciprocal(dinv2[:, 0:1], bank[5][:, 64:65])
                    sVE.inc(v.reciprocal(dinv2[:, 1:2], bank[7][:, 64:65]))
                on("vector", b_m4r)
                m4r_ve = sVE.n

                def b_m4s(v, m4r_ve=m4r_ve):
                    if hp_te_free[0]:
                        v.wait_ge(*hp_te_free[0])
                    v.drain()
                    sBKC[5].inc(v.tensor_scalar(headp[:, 0:64], bank[5][:, 0:64],
                                                dinv2[:, 0:1], None, ALU.mult))
                    sBKC[7].inc(v.tensor_scalar(headp[:, 64:128], bank[7][:, 0:64],
                                                dinv2[:, 1:2], None, ALU.mult))
                    sVE.inc(v.drain())
                on("vector", b_m4s)
                m4s_ae = sVE.n
                m4s_prev_ae[0] = m4s_ae

                def b_htr(t, m4s_ae=m4s_ae):
                    t.wait_ge(sVE.h, m4s_ae)
                    bfree(t, 6)
                    t.transpose(bkb[6][:, 0:128], headp[:], identb[:])
                    sBKP[6].inc(t.last)
                    hp_te_free[0] = (sBKP[6].h, sBKP[6].n)
                on("tensor", b_htr)

                def b_hev(v, pair=pair, tc=tc):
                    sBKP[6].wait(v)
                    if pair == 0 and tc == 0:
                        v.wait_ge(sTE.h, A_BAR)
                    v.drain()
                    sBKC[6].inc(v.tensor_copy(
                        attn_feat[:, pair * TC + tc * 128: pair * TC + (tc + 1) * 128],
                        bkb[6][:, 0:128]))
                    sVE.inc(v.drain())
                on("vector", b_hev)

            wq_te_free[par] = (sBKP[1].h, sBKP[1].n)
        B_VE = sVE.n

        def b2_bar(t):
            sTE.inc(t.drain())
        on("tensor", b2_bar)
        B2_BAR = sTE.n

        def b2_wo(g):
            g.wait_ge(sTE.h, B2_BAR)
            sWL.inc(g.dma_start(out=wo_sb.rearrange("p (c j) -> p c j", j=D),
                                in_=wo_d.rearrange("(c p) j -> p c j", p=128)), 16)
        on("gpsimd", b2_wo)
        WO_WL = sWL.n

        def b2_bo(t):
            bfree(t, 7)
            t.matmul(bank[7][:, 0:512], onesrowf[:, :], bo_row[:, 0:512], start=True, stop=True)
            sBKP[7].inc(t.last)
            bfree(t, 5)
            t.matmul(bank[5][:, 0:512], onesrowf[:, :], bo_row[:, 512:1024], start=True, stop=True)
            sBKP[5].inc(t.last)
        on("tensor", b2_bo)

        def b2_bev(v):
            sBKP[7].wait(v)
            sBKP[5].wait(v)
            v.drain()
            sBKC[7].inc(v.tensor_copy(borep[:, 0:512], bank[7][:, 0:512]))
            sBKC[5].inc(v.tensor_copy(borep[:, 512:1024], bank[5][:, 0:512]))
            sVE.inc(v.drain())
        on("vector", b2_bev)
        BOREP_VE = sVE.n

        y1_ve = [0] * NT
        y1_st_n = [0] * NT
        for tc in range(NT):
            par = tc % 2

            def b2_xr(s, tc=tc, par=par):
                if tc >= 2:
                    s.wait_ge(sVE.h, y1_ve[tc - 2])
                elif tc == 0:
                    s.wait_ge(sVE.h, P0_VE)
                sXR[par].inc(s.dma_start(out=xrbuf[par],
                                         in_=x_d[tc * 128:(tc + 1) * 128, :]), 16)
            on("sync", b2_xr)

            def b2_m5(t, tc=tc):
                if tc == 0:
                    t.wait_ge(sWL.h, WO_WL)
                    t.wait_ge(sVE.h, B_VE)
                bfree(t, 0)
                bfree(t, 1)
                for hp in range(NP):
                    t.matmul(bank[0][:, :],
                             attn_feat[:, hp * TC + tc * 128: hp * TC + (tc + 1) * 128],
                             wo_sb[:, hp * D: hp * D + 512],
                             start=(hp == 0), stop=(hp == NP - 1))
                sBKP[0].inc(t.last)
                for hp in range(NP):
                    t.matmul(bank[1][:, :],
                             attn_feat[:, hp * TC + tc * 128: hp * TC + (tc + 1) * 128],
                             wo_sb[:, hp * D + 512: hp * D + 1024],
                             start=(hp == 0), stop=(hp == NP - 1))
                sBKP[1].inc(t.last)
            on("tensor", b2_m5)

            def b2_ye(v, tc=tc, par=par):
                sBKP[0].wait(v)
                sBKP[1].wait(v)
                sXR[par].wait(v)
                if tc <= 1:
                    v.wait_ge(sVE.h, BOREP_VE)
                    v.wait_ge(sTE.h, B2_BAR)
                if tc >= 2:
                    v.wait_ge(sSY1.h, y1_st_n[tc - 1])
                v.drain()
                v.tensor_tensor(y1buf[par][:, 0:512], bank[0][:, :], borep[:, 0:512], ALU.add)
                sBKC[0].inc(v.drain())
                v.tensor_tensor(y1buf[par][:, 512:1024], bank[1][:, :],
                                borep[:, 512:1024], ALU.add)
                sBKC[1].inc(v.drain())
                v.tensor_tensor(y1buf[par], y1buf[par], xrbuf[par], ALU.add)
                sVE.inc(v.drain())
                y1_ve[tc] = sVE.n
            on("vector", b2_ye)

            def b2_st(s, tc=tc, par=par):
                s.wait_ge(sVE.h, y1_ve[tc])
                sSY1.inc(s.dma_start(out=y1_dram[tc * 128:(tc + 1) * 128, :],
                                     in_=y1buf[par]), 16)
                y1_st_n[tc] = sSY1.n
            on("sync", b2_st)

        def c_bar(t):
            sTE.inc(t.drain())
        on("tensor", c_bar)
        C_PE_BAR = sTE.n

        if trunc == 3:
            _stub_final()
            return nc
        if debug >= 3:
            def dbg_y1(g):
                g.wait_ge(sSY1.h, sSY1.n)
                sDBG2.inc(g.dma_start(out=dbg["y1"][:, :], in_=y1_dram[:, :]), 16)
            on("gpsimd", dbg_y1)

        def c_w1(g):
            g.wait_ge(sTE.h, B2_BAR)
            sW1.inc(g.dma_start(out=w1_sb[:, 0:DC * F].rearrange("p (c f) -> p c f", f=F),
                                in_=w1_d.rearrange("(c p) f -> p c f", p=128)), 16)
        on("gpsimd", c_w1)
        W1_WL = sW1.n

        def c_w2(g):
            g.wait_ge(sTE.h, C_PE_BAR)
            sWL.inc(g.dma_start(out=w2_sb.rearrange("p (c j) -> p c j", j=D),
                                in_=w2_d.rearrange("(c p) j -> p c j", p=128)), 16)
        on("gpsimd", c_w2)
        W2_WL = sWL.n

        def c_rld(s):
            sCC3.wait(s)
            s.wait_ge(sSY1.h, sSY1.n)
            SR = TC // TBLF
            for r in range(NCORE):
                sMGL.inc(s.dma_start(
                    out=eidxall[r * SR:(r + 1) * SR, :],
                    in_=msgB_out[r:r + 1, 16:16 + TC].rearrange(
                        "a (s f) -> (a s) f", f=TBLF)), 16)
                sMGL.inc(s.dma_start(
                    out=ptokall[r * SR:(r + 1) * SR, :],
                    in_=msgB_out[r:r + 1, 16 + TC:16 + 2 * TC].rearrange(
                        "a (s f) -> (a s) f", f=TBLF)), 16)
            sMGL.inc(s.dma_start(out=ca8, in_=msgB_out[:, 1:1 + E]), 16)
        on("sync", c_rld)
        RLD_M = sMGL.n

        def c_rk1(v):
            v.wait_ge(sMGL.h, RLD_M)
            v.wait_ge(sTE.h, C_PE_BAR)
            v.drain()
            v.tensor_scalar(maskE, eidxall, cidrep[:, 0:1], None, ALU.is_equal)
            v.drain()
            v.tensor_copy(maskEb, maskE)
            v.tensor_copy(scrA, maskE)
            v.drain()
            d = 1
            src, dst = scrA, scrB
            while d < TBLF:
                v.tensor_copy(dst[:, 0:d], src[:, 0:d])
                v.tensor_tensor(dst[:, d:TBLF], src[:, d:TBLF], src[:, 0:TBLF - d], ALU.add)
                v.drain()
                src, dst = dst, src
                d *= 2
            v.tensor_copy(rowtotb, src[:, TBLF - 1:TBLF])
            v.drain()
            sVE.inc(v.tensor_tensor(rankE, src, maskE, ALU.subtract))
        on("vector", c_rk1)
        RK1_VE = sVE.n

        def c_rk2(t):
            t.wait_ge(sVE.h, RK1_VE)
            bfree(t, 5)
            t.matmul(bank[5][:, 0:1], Lb[:, :], rowtotb, start=True, stop=True)
            sBKP[5].inc(t.last)
        on("tensor", c_rk2)

        def c_rk3(v):
            sBKP[5].wait(v)
            v.drain()
            sBKC[5].inc(v.tensor_scalar(rankE, rankE, bank[5][:, 0:1], None, ALU.add))
            v.drain()
            v.tensor_scalar(scrA, maskE, -BIGOOB, BIGOOB, ALU.mult, ALU.add)
            v.drain()
            v.tensor_tensor(rankE, rankE, scrA, ALU.add)
            v.drain()
            scAi = scrA.bitcast(I32)
            scBi = scrB.bitcast(I32)
            v.tensor_copy(scAi, rankE)
            v.drain()
            v.tensor_scalar(scBi, scAi, 63, None, ALU.bitwise_and)
            v.drain()
            v.tensor_scalar(scAi, scAi, 6, None, ALU.arith_shift_right)
            v.drain()
            v.tensor_copy(loE, scBi)
            v.drain()
            sVE.inc(v.tensor_copy(hiE, scAi))
        on("vector", c_rk3)
        RK3_VE = sVE.n

        for (srcT, dstT) in ((hiE, hiT), (loE, loT), (ptokall, gTT)):
            def c_tt(t, srcT=srcT):
                sVE.wait(t)
                bfree(t, 3)
                for j in range(NTT):
                    t.transpose(bank[3][:, j * 128:(j + 1) * 128],
                                srcT[:, j * 128:(j + 1) * 128], identf[:])
                sBKP[3].inc(t.last)
            on("tensor", c_tt)

            def c_tv(v, dstT=dstT):
                sBKP[3].wait(v)
                v.drain()
                for j in range(NTT):
                    v.tensor_copy(dstT[:, j * 128:(j + 1) * 128],
                                  bank[3][:, j * 128:(j + 1) * 128])
                sBKC[3].inc(v.drain())
                sVE.inc(v.drain())
            on("vector", c_tv)
        TBLT_VE = sVE.n

        oh_te = [0]
        for c in range(NCH):
            pidx = (c * 128) // TBLF
            tj = ((c * 128) % TBLF) // 128
            ccol = tj * 128 + pidx

            def c_oh(v, c=c, ccol=ccol):
                if c == 0:
                    v.wait_ge(sVE.h, TBLT_VE)
                else:
                    v.wait_ge(sTE.h, oh_te[0])
                v.drain()
                v.tensor_scalar(ohhi[:], hiio[:], hiT[:, ccol:ccol + 1], None, ALU.is_equal)
                v.tensor_scalar(ohblk[:, 0:64], hiio[:], loT[:, ccol:ccol + 1],
                                None, ALU.is_equal)
                v.drain()
                v.tensor_scalar(ohblk[:, 64:128], ohblk[:, 0:64], float(c >> 1),
                                None, ALU.mult)
                v.tensor_scalar(ohblk[:, 128:192], ohblk[:, 0:64],
                                plo1[:] if (c & 1) else piota[:], None, ALU.mult)
                v.tensor_scalar(ohblk[:, 192:256], ohblk[:, 0:64],
                                gTT[:, ccol:ccol + 1], None, ALU.mult)
                sVE.inc(v.drain())
            on("vector", c_oh)
            oh_ve = sVE.n

            def c_ohm(t, c=c, oh_ve=oh_ve):
                t.wait_ge(sVE.h, oh_ve)
                if c == 0:
                    bfree(t, 5)
                t.matmul(bank[5][0:64, 0:256], ohhi[:, :], ohblk[:, :],
                         start=(c == 0), stop=(c == NCH - 1))
                if c == NCH - 1:
                    sBKP[5].inc(t.last)
                else:
                    sTE.inc(t.last)
                    oh_te[0] = sTE.n
            on("tensor", c_ohm)

        def c_dt1(v):
            sBKP[5].wait(v)
            v.drain()
            sBKC[5].inc(v.tensor_copy(dtab[:, :], bank[5][0:64, 0:256]))
            v.drain()
            v.tensor_scalar(dfull[:, :], dtab[:, 64:128], 256.0, None, ALU.mult)
            v.drain()
            v.tensor_tensor(dfull[:, :], dfull[:, :], dtab[:, 128:192], ALU.add)
            v.drain()
            v.tensor_scalar(scrA[0:64, 0:64], dtab[:, 0:64], -BIGOOB, BIGOOB,
                            ALU.mult, ALU.add)
            v.drain()
            sVE.inc(v.tensor_tensor(dfull[:, :], dfull[:, :], scrA[0:64, 0:64], ALU.add))
        on("vector", c_dt1)
        DT1_VE = sVE.n

        if debug >= 4:
            def dbg_dt(s):
                s.wait_ge(sVE.h, DT1_VE)
                sDBG.inc(s.dma_start(out=dbg["dtab"][:, :], in_=dtab[:, :]), 16)
            on("sync", dbg_dt)

        def c_dt2(t):
            t.wait_ge(sVE.h, DT1_VE)
            bfree(t, 7)
            t.transpose(bank[7][0:64, 0:64], dfull[:, :], identf[0:64, 0:64])
            t.transpose(bank[7][0:64, 64:128], dtab[:, 192:256], identf[0:64, 0:64])
            sBKP[7].inc(t.last)
        on("tensor", c_dt2)

        def c_dt3(v):
            sBKP[7].wait(v)
            v.drain()
            v.tensor_copy(dfT[:, :], bank[7][0:64, 0:64])
            sBKC[7].inc(v.tensor_copy(gcq[:, :], bank[7][0:64, 64:128]))
            v.drain()
            v.tensor_copy(dfTi[:, :], dfT[:, :])
            sVE.inc(v.tensor_copy(gcol[0:64, 0:CAP // 128],
                                  gcq[:, 0:CAP // 64][:, 0::2]))
        on("vector", c_dt3)
        DT3_VE = sVE.n

        def c_gc(g):
            g.wait_ge(sVE.h, DT3_VE)
            sGC.inc(g.dma_start(out=gcol[64:128, 0:CAP // 128],
                                in_=gcq[:, 0:CAP // 64][:, 1::2],
                                allow_slow_non_contiguous=True), 16)
        on("gpsimd", c_gc)

        xe_te_free = [None, None]
        hb_te_mark = [0]
        ye_st_mark = [0, 0]

        for gi in range(NG):
            xet_marks = []
            for j2 in range(2):
                jj = gi * 2 + j2

                def c_gat(g, gi=gi, j2=j2, jj=jj):
                    if gi == 0 and j2 == 0:
                        sCC1.wait(g)
                        g.wait_ge(sVE.h, DT3_VE)
                        g.wait_ge(sW1.h, W1_WL)
                    if xe_te_free[j2]:
                        g.wait_ge(*xe_te_free[j2])
                    sGE.inc(g.memset(xe_tok[j2], 0.0))
                    sGE.wait(g)
                    for half in range(2):
                        sGA[j2].inc(g.indirect_dma_start(
                            out=xe_tok[j2][half * 64:(half + 1) * 64, :],
                            out_offset=None,
                            in_=xnall[:, :],
                            in_offset=bass.IndirectOffsetOnAxis(
                                ap=dfTi[:, 2 * jj + half: 2 * jj + half + 1], axis=0),
                            bounds_check=bc_reg[0], oob_is_err=False), 16)
                on("gpsimd", c_gat)
                ga_n = sGA[j2].n

                def c_xtr(t, j2=j2, ga_n=ga_n):
                    t.wait_ge(sGA[j2].h, ga_n)
                    bfree(t, 4)
                    for c in range(DC):
                        t.transpose(bkb[4][:, c * 128:(c + 1) * 128],
                                    xe_tok[j2][:, c * 128:(c + 1) * 128], identb[:])
                    sBKP[4].inc(t.last)
                    xe_te_free[j2] = (sBKP[4].h, sBKP[4].n)
                on("tensor", c_xtr)

                def c_xte(v, j2=j2, gi=gi):
                    sBKP[4].wait(v)
                    if gi == 0:
                        v.wait_ge(sTE.h, C_PE_BAR)
                    v.drain()
                    for c in range(DC):
                        v.tensor_copy(xeT[:, c * 256 + j2 * 128: c * 256 + j2 * 128 + 128],
                                      bkb[4][:, c * 128:(c + 1) * 128])
                    sBKC[4].inc(v.drain())
                    sVE.inc(v.drain())
                on("vector", c_xte)
            xet_ve = sVE.n

            for fc in range(FC):
                bm = fc % 2

                def c_m6(t, gi=gi, fc=fc, bm=bm, xet_ve=xet_ve):
                    if fc == 0:
                        t.wait_ge(sVE.h, xet_ve)
                        if gi == 0:
                            t.wait_ge(sW1.h, W1_WL)
                    bfree(t, bm)
                    for c in range(DC):
                        t.matmul(bank[bm][:, 0:256],
                                 w1_sb[:, c * F + fc * 128: c * F + (fc + 1) * 128],
                                 xeT[:, c * 256:(c + 1) * 256],
                                 start=(c == 0), stop=(c == DC - 1))
                    sBKP[bm].inc(t.last)
                on("tensor", c_m6)

                def c_rl(v, gi=gi, fc=fc, bm=bm):
                    sBKP[bm].wait(v)
                    if gi == 0 and fc < 2:
                        v.wait_ge(sTE.h, C_PE_BAR)
                    if hb_te_mark[0] and fc < 2:
                        v.wait_ge(sTE.h, hb_te_mark[0])
                    sBKC[bm].inc(v.tensor_scalar(hbuf[:, fc * 256:(fc + 1) * 256],
                                                 bank[bm][:, 0:256], 0.0, None, ALU.max))
                    sVE.inc(v.drain())
                on("vector", c_rl)
            h_ve = sVE.n
            h_ae = sAE.n

            for sc2 in range(2):
                for dot in range(2):
                    bm7 = 2 + dot

                    def c_m7(t, gi=gi, sc2=sc2, dot=dot, bm7=bm7, h_ve=h_ve, h_ae=h_ae):
                        if gi == 0 and sc2 == 0 and dot == 0:
                            t.wait_ge(sWL.h, W2_WL)
                        t.wait_ge(sVE.h, h_ve)
                        bfree(t, bm7)
                        for fc in range(FC):
                            t.matmul(bank[bm7][:, :],
                                     hbuf[:, fc * 256 + sc2 * 128: fc * 256 + sc2 * 128 + 128],
                                     w2_sb[:, fc * D + dot * 512: fc * D + (dot + 1) * 512],
                                     start=(fc == 0), stop=(fc == FC - 1))
                        sBKP[bm7].inc(t.last)
                        if gi == 0 and sc2 == 0 and dot == 0:
                            pass
                    on("tensor", c_m7)
                    bkp_n = sBKP[bm7].n

                    def c_yev(v, gi=gi, sc2=sc2, dot=dot, bm7=bm7, bkp_n=bkp_n):
                        v.wait_ge(sBKP[bm7].h, bkp_n)
                        if gi == 0 and sc2 == 0 and dot == 0:
                            sGC.wait(v)
                            v.wait_ge(sWL.h, W2_WL)
                            v.wait_ge(sTE.h, C_PE_BAR)
                        if ye_st_mark[sc2] and dot == 0:
                            v.wait_ge(sSYE.h, sSYE.n)
                        sBKC[bm7].inc(v.tensor_scalar(
                            yebuf[sc2][:, dot * 512:(dot + 1) * 512],
                            bank[bm7][:, :],
                            gcol[:, gi * 2 + sc2: gi * 2 + sc2 + 1], None, ALU.mult))
                        sVE.inc(v.drain())
                    on("vector", c_yev)
                ye_ae = sVE.n

                def c_yst(s, gi=gi, sc2=sc2, ye_ae=ye_ae):
                    s.wait_ge(sVE.h, ye_ae)
                    sSYE.inc(s.dma_start(
                        out=ye_dram[gi * 256 + sc2 * 128: gi * 256 + (sc2 + 1) * 128, :],
                        in_=yebuf[sc2]), 16)
                on("sync", c_yst)
                ye_st_mark[sc2] = sSYE.n

            def c_hbm(t):
                sTE.inc(t.nop())
                hb_te_mark[0] = sTE.n
            on("tensor", c_hbm)

        def c_agy(g):
            g.wait_ge(sSYE.h, sSYE.n)
            sCC4.inc(g.collective_compute(
                "AllGather", ALU.bypass, replica_groups=[list(range(NCORE))],
                ins=[ye_dram.opt()], outs=[yeall.opt()]))
        on("gpsimd", c_agy)
        if trunc == 4:
            _stub_final()
            return nc

        def f_cnt(t):
            bfree(t, 5)
            t.matmul(bank[5][0:1, 0:E * NT], ones128b[:, :], maskbig[:],
                     start=True, stop=True)
            sBKP[5].inc(t.last)
        on("tensor", f_cnt)

        def f_cntv(v):
            sBKP[5].wait(v)
            v.drain()
            sBKC[5].inc(v.tensor_copy(cnte[:], bank[5][0:1, 0:E * NT]))
            sVE.inc(v.drain())
        on("vector", f_cntv)

        def f_cpre(v):
            v.drain()
            v.memset(cntp[:], 0.0)
            v.drain()
            ve = cntp.rearrange("a (e t) -> a e t", t=NT)
            se = cnte.rearrange("a (e t) -> a e t", t=NT)
            v.tensor_copy(ve[:, :, 1:NT], se[:, :, 0:NT - 1])
            v.drain()
            d = 1
            src, dst = cntp, cnte
            while d < NT:
                sv = src.rearrange("a (e t) -> a e t", t=NT)
                dv = dst.rearrange("a (e t) -> a e t", t=NT)
                v.tensor_copy(dv[:, :, 0:d], sv[:, :, 0:d])
                v.tensor_tensor(dv[:, :, d:NT], sv[:, :, d:NT], sv[:, :, 0:NT - d], ALU.add)
                v.drain()
                src, dst = dst, src
                d *= 2
            if src is not cntp:
                v.tensor_copy(cntp[:], cnte[:])
                v.drain()
            v.tensor_scalar(ltm8[:, :], piota[0:NCORE, 0:1], cidrep[0:NCORE, 0:1],
                            None, ALU.is_lt)
            v.drain()
            v.wait_ge(sMGL.h, RLD_M)
            sVE.inc(v.tensor_scalar(ca8[:, :], ca8[:, :], ltm8[:, 0:1], None, ALU.mult))
        on("vector", f_cpre)
        FC_VE = sVE.n

        def f_offs(t):
            t.wait_ge(sVE.h, FC_VE)
            bfree(t, 7)
            t.matmul(bank[7][0:1, 32:32 + E], onescolf[0:NCORE, 0:1], ca8[:, :],
                     start=True, stop=True)
            sBKP[7].inc(t.last)
        on("tensor", f_offs)

        def f_ov(v):
            sBKP[7].wait(v)
            v.drain()
            sBKC[7].inc(v.tensor_copy(offs8[:, :], bank[7][0:1, 32:32 + E]))
            v.drain()
            sVE.inc(v.memset(rmine, 0.0))
        on("vector", f_ov)
        FO_VE = sVE.n

        for e in range(E):
            def f_re(v, e=e):
                if e == 0:
                    v.wait_ge(sVE.h, FO_VE)
                v.drain()
                sVE.inc(v.tensor_scalar(rowe[:, :], cntp[:, e * NT:(e + 1) * NT],
                                        offs8[:, e:e + 1], None, ALU.add))
            on("vector", f_re)
            re_ve = sVE.n

            def f_rm(t, e=e, re_ve=re_ve):
                t.wait_ge(sVE.h, re_ve)
                bfree(t, 5)
                t.matmul(bank[5][:, 0:NT], Lb[:, :], maskbig[:, e * NT:(e + 1) * NT],
                         start=True, stop=True)
                sBKP[5].inc(t.last)
                bfree(t, 7)
                t.matmul(bank[7][:, 32:32 + NT], onesrowf[:, :], rowe[:, :],
                         start=True, stop=True)
                sBKP[7].inc(t.last)
            on("tensor", f_rm)

            def f_rv(v, e=e):
                sBKP[5].wait(v)
                sBKP[7].wait(v)
                v.drain()
                sBKC[5].inc(v.tensor_copy(le_sb[:], bank[5][:, 0:NT]))
                v.drain()
                sBKC[7].inc(v.tensor_tensor(le_sb[:], le_sb[:], bank[7][:, 32:32 + NT],
                                            ALU.add))
                v.drain()
                v.tensor_tensor(rk_u, maskf[:, e * NT:(e + 1) * NT], le_sb[:], ALU.mult)
                v.drain()
                v.tensor_tensor(rk_t, rmine, rk_u, ALU.add)
                v.drain()
                sVE.inc(v.tensor_copy(rmine, rk_t))
            on("vector", f_rv)
        RM_VE = sVE.n

        if debug >= 4:
            def dbg_rm(s):
                s.wait_ge(sVE.h, RM_VE)
                sDBG.inc(s.dma_start(out=dbg["rankm"][:, :], in_=rmine), 16)
            on("sync", dbg_rm)

        def f_idx(v):
            v.drain()
            v.tensor_scalar(rk_u, rmine, float(CAP), None, ALU.is_ge)
            v.drain()
            v.tensor_scalar(rk_t, rk_u, BIGOOB, None, ALU.mult)
            v.drain()
            v.tensor_scalar(rk_s, eidx_mine[:, :], float(CAP), None, ALU.mult)
            v.drain()
            v.tensor_tensor(rk_s, rk_s, rmine, ALU.add)
            v.drain()
            v.tensor_tensor(rk_s, rk_s, rk_t, ALU.add)
            v.drain()
            sVE.inc(v.tensor_copy(idx2i[:, :], rk_s))
        on("vector", f_idx)
        IDX_VE = sVE.n

        fin_ve = [0] * NT
        out_st_n = [0] * NT
        for tc in range(NT):
            par = tc % 2

            def f_y1l(s, tc=tc, par=par):
                if tc >= 2:
                    s.wait_ge(sVE.h, fin_ve[tc - 2])
                elif tc == 0:
                    s.wait_ge(sVE.h, RM_VE)
                sY1L[par].inc(s.dma_start(out=xrbuf[par],
                                          in_=y1_dram[tc * 128:(tc + 1) * 128, :]), 16)
            on("sync", f_y1l)

            def f_gye(g, tc=tc, par=par):
                if tc == 0:
                    sCC4.wait(g)
                    g.wait_ge(sVE.h, IDX_VE)
                    g.wait_ge(sTE.h, C_PE_BAR)
                if tc >= 2:
                    g.wait_ge(sVE.h, fin_ve[tc - 2])
                sGE.inc(g.memset(yegb[par], 0.0))
                sGE.wait(g)
                sGB[par].inc(g.indirect_dma_start(
                    out=yegb[par], out_offset=None, in_=yeall[:, :],
                    in_offset=bass.IndirectOffsetOnAxis(ap=idx2i[:, tc:tc + 1], axis=0),
                    bounds_check=bc_reg[0], oob_is_err=False), 16)
            on("gpsimd", f_gye)
            gb_n = sGB[par].n

            def f_add(v, tc=tc, par=par, gb_n=gb_n):
                sY1L[par].wait(v)
                v.wait_ge(sGB[par].h, gb_n)
                if tc >= 2:
                    v.wait_ge(sOUT.h, out_st_n[tc - 1])
                v.drain()
                sVE.inc(v.tensor_tensor(y1buf[par], xrbuf[par], yegb[par], ALU.add))
                fin_ve[tc] = sVE.n
            on("vector", f_add)

            def f_st(s, tc=tc, par=par):
                s.wait_ge(sVE.h, fin_ve[tc])
                sOUT.inc(s.dma_start(out=y_d[tc * 128:(tc + 1) * 128, :],
                                     in_=y1buf[par]), 16)
                out_st_n[tc] = sOUT.n
            on("sync", f_st)

        def f_wait(s):
            s.wait_ge(sOUT.h, sOUT.n)
            if dbg:
                s.wait_ge(sDBG.h, sDBG.n)
                s.wait_ge(sDBG2.h, sDBG2.n)
        on("sync", f_wait)

    return nc


def _stage_inputs(inputs, N=None):
    N = N_ if N is None else N
    T = B_ * N
    TCl = T // NCORE
    x = np.ascontiguousarray(np.asarray(inputs["L0_attn_x"], dtype=np.float32)
                             .reshape(T, D_))
    base = {
        "wq": np.ascontiguousarray(np.asarray(inputs["L0_attn_Wq"], dtype=np.float32)),
        "wk": np.ascontiguousarray(np.asarray(inputs["L0_attn_Wk"], dtype=np.float32)),
        "wv": np.ascontiguousarray(np.asarray(inputs["L0_attn_Wv"], dtype=np.float32)),
        "wo": np.ascontiguousarray(np.asarray(inputs["L0_attn_Wo"], dtype=np.float32)),
        "bo": np.ascontiguousarray(np.asarray(inputs["L0_attn_bo"], dtype=np.float32)
                                   .reshape(1, D_)),
        "pmat": np.ascontiguousarray(np.asarray(inputs["L0_attn_P"], dtype=np.float32)),
        "wr": np.ascontiguousarray(np.asarray(inputs["L0_ffn_Wr"], dtype=np.float32)),
    }
    w1 = np.asarray(inputs["L0_ffn_W1"], dtype=np.float32)
    w2 = np.asarray(inputs["L0_ffn_W2"], dtype=np.float32)
    in_maps = []
    for c in range(NCORE):
        m = dict(base)
        m["x"] = np.ascontiguousarray(x[c * TCl:(c + 1) * TCl])
        m["w1"] = np.ascontiguousarray(w1[c])
        m["w2"] = np.ascontiguousarray(w2[c])
        m["cid"] = np.array([[float(c)]], dtype=np.float32)
        in_maps.append(m)
    return in_maps


def kernel(**inputs):
    key = (DEBUG,)
    if key not in _CACHE:
        _CACHE[key] = build(debug=DEBUG)
    nc = _CACHE[key]
    in_maps = _stage_inputs(inputs)
    res = run_bass_kernel_spmd(nc, in_maps, core_ids=list(range(NCORE)))
    out = np.concatenate([np.asarray(r["y"]) for r in res.results], axis=0)
    return out.reshape(B_, N_, D_)
